# version 12
# baseline (speedup 1.0000x reference)
import numpy as np
import ml_dtypes
from contextlib import ExitStack

import concourse.bass as bass
import concourse.mybir as mybir
from concourse.bass_utils import run_bass_kernel_spmd

F32 = mybir.dt.float32
BF16 = mybir.dt.bfloat16
AF = mybir.ActivationFunctionType
ALU = mybir.AluOpType
AX = mybir.AxisListType

B, S, D = 4, 4096, 1024
NH, DH = 16, 64
E, DFF = 8, 2048
NT = 16
NT_SEQ = 32
HC = 8
FC = DFF // 128
GC = DFF // 128
NEG = -1.0e9
_DEBUG = False


def _sub_overlap(a, b):
    if a is None or b is None:
        return True
    if not isinstance(a, tuple):
        a = (a,)
    if not isinstance(b, tuple):
        b = (b,)
    for x, y in zip(a, b):
        if x is None or y is None:
            continue
        if x != y:
            return False
    return True


class Em:
    ENGS = ("t", "v", "s", "g", "q")

    NDMA = {"q": 8, "g": 1}

    def __init__(self, nc):
        self.nc = nc
        self.qs = {e: [] for e in self.ENGS}
        self.sem_names = (["t", "v", "s", "g", "q"]
                          + [f"dq{i}" for i in range(self.NDMA["q"])]
                          + [f"dg{i}" for i in range(self.NDMA["g"])])
        self.dma_n = {"q": 0, "g": 0}
        self.counts = {n: 0 for n in self.sem_names}
        self.wm = {e: {} for e in self.ENGS}
        self.W = {}
        self.R = {}
        self._sems = None

    def _edges_read(self, key):
        name, sub = key
        out = []
        for s2, ev in self.W.get(name, {}).items():
            if _sub_overlap(sub, s2):
                out.append(ev)
        return out

    def _edges_write(self, key):
        name, sub = key
        out = list(self._edges_read(key))
        for s2, evs in self.R.get(name, {}).items():
            if _sub_overlap(sub, s2):
                out.extend(evs)
        return out

    def _note_read(self, key, ev):
        name, sub = key
        self.R.setdefault(name, {}).setdefault(sub, []).append(ev)

    def _note_write(self, key, ev):
        name, sub = key
        self.W.setdefault(name, {})[sub] = ev
        rd = self.R.get(name)
        if rd:
            dead = [s2 for s2 in rd if _sub_overlap(sub, s2)]
            for s2 in dead:
                del rd[s2]

    def op(self, eng, fn, rd=(), wr=(), dma=False, wmode="full"):
        need = []
        for k in rd:
            need.extend(self._edges_read(k))
        if wmode in ("full", "check"):
            for k in wr:
                need.extend(self._edges_write(k))
        waits = []
        wm = self.wm[eng]
        for sem, val, clock in need:
            if wm.get(sem, 0) < val:
                waits.append((sem, val))
                for s2, v2 in clock.items():
                    if wm.get(s2, 0) < v2:
                        wm[s2] = v2
                wm[sem] = max(wm.get(sem, 0), val)
        if dma:
            pref = {"q": "dq", "g": "dg"}[eng]
            n = self.dma_n[eng]
            self.dma_n[eng] = n + 1
            sem = f"{pref}{n % self.NDMA[eng]}"
            inc = 16
            prev = self.counts[sem]
            if wm.get(sem, 0) < prev:
                waits.append((sem, prev))
                wm[sem] = prev
        else:
            sem, inc = eng, 1
        self.counts[sem] += inc
        ev = (sem, self.counts[sem], dict(wm, **{sem: self.counts[sem]}))
        sems_ref = lambda: self._sems

        def closure(epx, _w=tuple(waits), _fn=fn, _sem=sem, _inc=inc, _sr=sems_ref):
            sems = _sr()
            for s2, v2 in _w:
                epx.wait_ge(sems[s2], v2)
            _fn(epx).then_inc(sems[_sem], _inc)

        self.qs[eng].append(closure)
        for k in rd:
            self._note_read(k, ev)
        if wmode in ("full", "record"):
            for k in wr:
                self._note_write(k, ev)
        return ev

    def wait_all(self, eng):
        snap = dict(self.counts)
        wm = self.wm[eng]
        waits = [(s2, v2) for s2, v2 in snap.items() if v2 > wm.get(s2, 0)]
        for s2, v2 in waits:
            wm[s2] = v2
        sems_ref = lambda: self._sems

        def closure(epx, _w=tuple(waits), _sr=sems_ref):
            sems = _sr()
            for s2, v2 in _w:
                epx.wait_ge(sems[s2], v2)

        self.qs[eng].append(closure)

    def barrier(self):
        for e in self.ENGS:
            self.wait_all(e)

    def build(self, block):
        qs = self.qs
        block.sync(lambda e: [f(e) for f in qs["q"]])
        block.gpsimd(lambda e: [f(e) for f in qs["g"]])
        block.tensor(lambda e: [f(e) for f in qs["t"]])
        block.vector(lambda e: [f(e) for f in qs["v"]])
        block.scalar(lambda e: [f(e) for f in qs["s"]])


def build_program():
    nc = bass.Bass("TRN2", target_bir_lowering=False)

    din = {}
    din["x_seq"] = nc.dram_tensor("x_seq", [S, D], F32, kind="ExternalInput").ap()
    din["x_own"] = nc.dram_tensor("x_own", [NT, 128, D], F32, kind="ExternalInput").ap()
    din["wqkv"] = nc.dram_tensor("wqkv", [D, 3 * D], F32, kind="ExternalInput").ap()
    din["gate_w"] = nc.dram_tensor("gate_w", [D, E], F32, kind="ExternalInput").ap()
    din["gate_b"] = nc.dram_tensor("gate_b", [1, E], F32, kind="ExternalInput").ap()
    for nm, shp in (("w1", [E, D, DFF]), ("w2", [E, D, DFF]), ("wg", [E, DFF, DFF]),
                    ("w3", [E, DFF, D]), ("b1", [E, DFF]), ("b2", [E, DFF]),
                    ("bg", [E, DFF]), ("b3", [E, D])):
        din[nm] = nc.dram_tensor(nm, shp, F32, kind="ExternalInput").ap()
    din["mask"] = nc.dram_tensor("mask", [128, 1024], F32, kind="ExternalInput").ap()
    din["ident_bf"] = nc.dram_tensor("ident_bf", [128, 128], BF16, kind="ExternalInput").ap()
    din["ident_f"] = nc.dram_tensor("ident_f", [128, 128], F32, kind="ExternalInput").ap()
    out_hbm = nc.dram_tensor("out", [NT, 128, D], F32, kind="ExternalOutput").ap()
    if _DEBUG:
        dbg_x2a = nc.dram_tensor("dbg_x2a", [NT, 128, D], F32, kind="ExternalOutput").ap()
        dbg_we = nc.dram_tensor("dbg_we", [128, NT * E], F32, kind="ExternalOutput").ap()
        dbg_xn2 = nc.dram_tensor("dbg_xn2", [128, HC * NT * 128], BF16, kind="ExternalOutput").ap()
        dbg_z = nc.dram_tensor("dbg_z", [128, FC * 1024], BF16, kind="ExternalOutput").ap()
        dbg_s = nc.dram_tensor("dbg_s", [128, GC * 1024], BF16, kind="ExternalOutput").ap()
        dbg_x2e = nc.dram_tensor("dbg_x2e", [NT, 128, D], F32, kind="ExternalOutput").ap()

    scr = {}
    scr["qkv"] = nc.dram_tensor("scr_qkv", [D, 3 * D], BF16, kind="Internal").ap()
    scr["w1"] = nc.dram_tensor("scr_w1", [E, D, DFF], BF16, kind="Internal").ap()
    scr["w2"] = nc.dram_tensor("scr_w2", [E, D, DFF], BF16, kind="Internal").ap()
    scr["wg"] = nc.dram_tensor("scr_wg", [E, DFF, DFF], BF16, kind="Internal").ap()
    scr["w3"] = nc.dram_tensor("scr_w3", [E, DFF, D], BF16, kind="Internal").ap()

    em = Em(nc)

    with ExitStack() as ctx:
        sb = lambda n, s_, dt: ctx.enter_context(nc.sbuf_tensor(n, s_, dt))

        x2 = sb("x2", [128, NT * D], F32)
        ident_bf = sb("identbf_sb", [128, 128], BF16)
        ident_f = sb("identf_sb", [128, 128], F32)
        mask_sb = sb("mask_sb", [128, 1024], F32)
        ones_sb = sb("ones_sb", [65, 256], BF16)
        gw_sb = sb("gw_sb", [128, HC * E], BF16)
        gb_sb = sb("gb_sb", [1, E], BF16)
        we_sb = sb("we_sb", [128, NT * E], F32)
        b1c = sb("b1c", [128, E * FC], F32)
        b2c = sb("b2c", [128, E * FC], F32)
        bgc = sb("bgc", [128, E * GC], F32)
        b3rows = sb("b3rows", [65, D], BF16)

        sems = {}
        for n in em.sem_names:
            sems[n] = ctx.enter_context(nc.semaphore(n))
        em._sems = sems

        em.op("g", lambda g: g.dma_start(out=scr["qkv"], in_=din["wqkv"]),
              wr=[("scr_qkv", None)], dma=True)
        em.op("g", lambda g: g.dma_start(out=gb_sb[:, :], in_=din["gate_b"]),
              wr=[("gb", None)], dma=True)
        em.op("g", lambda g: g.dma_start(
            out=gw_sb[:, :].rearrange("p (c e) -> p c e", c=HC),
            in_=din["gate_w"].rearrange("(c p) e -> p c e", p=128)),
            wr=[("gw", None)], dma=True)
        em.op("g", lambda g: g.memset(ones_sb[:, :], 1.0), wr=[("ones", None)])
        for e in range(E):
            for nm in ("w1", "w2", "wg", "w3"):
                em.op("g", lambda g, nm_=nm, e_=e: g.dma_start(
                    out=scr[nm_][e_], in_=din[nm_][e_]),
                    wr=[("scr_" + nm, e)], dma=True)
        em.op("q", lambda q: q.dma_start(
            out=x2[:, :].rearrange("p (a d) -> p a d", a=NT),
            in_=din["x_own"].rearrange("a p d -> p a d")),
            wr=[("x2", None)], dma=True)
        em.op("q", lambda q: q.dma_start(out=ident_bf[:, :], in_=din["ident_bf"]),
              wr=[("identbf", None)], dma=True)
        em.op("q", lambda q: q.dma_start(out=ident_f[:, :], in_=din["ident_f"]),
              wr=[("identf", None)], dma=True)
        em.op("q", lambda q: q.dma_start(out=mask_sb[:, :], in_=din["mask"]),
              wr=[("mask", None)], dma=True)

        with ExitStack() as actx:
            asb = lambda n, s_, dt: actx.enter_context(nc.sbuf_tensor(n, s_, dt))
            xn1T = asb("xn1T", [128, HC * S], BF16)
            xn1To = asb("xn1To", [128, HC * NT * 128], BF16)

            def emit_ln_front(src_fn, src_keys, st, st_name, slot, xnscr, xnscr_name,
                              junk, junk_name):
                c0 = slot * 8
                kst = lambda n_: (st_name, (slot, n_))
                em.op("v", lambda v: v.reduce_sum(st[:, c0:c0 + 1], src_fn(), axis=AX.X),
                      rd=src_keys, wr=[kst(0)])
                em.op("s", lambda s2: s2.activation(
                    junk[:, :], src_fn(), AF.Square, accum_out=st[:, c0 + 1:c0 + 2]),
                    rd=src_keys, wr=[(junk_name, None), kst(1)])
                em.op("v", lambda v: v.tensor_scalar(
                    st[:, c0 + 2:c0 + 3], st[:, c0:c0 + 1], 1.0 / D, None, ALU.mult),
                    rd=[kst(0)], wr=[kst(2)])
                em.op("v", lambda v: v.tensor_scalar(
                    st[:, c0 + 3:c0 + 4], st[:, c0 + 1:c0 + 2], 1.0 / D, 1e-5,
                    ALU.mult, ALU.add),
                    rd=[kst(1)], wr=[kst(3)])
                em.op("v", lambda v: v.scalar_tensor_tensor(
                    st[:, c0 + 4:c0 + 5], st[:, c0 + 2:c0 + 3], st[:, c0 + 2:c0 + 3],
                    st[:, c0 + 3:c0 + 4], ALU.mult, ALU.subtract),
                    rd=[kst(2), kst(3)], wr=[kst(4)])
                em.op("s", lambda s2: s2.activation(
                    st[:, c0 + 5:c0 + 6], st[:, c0 + 4:c0 + 5], AF.Sqrt, scale=-1.0),
                    rd=[kst(4)], wr=[kst(5)])
                em.op("v", lambda v: v.reciprocal(st[:, c0 + 6:c0 + 7], st[:, c0 + 5:c0 + 6]),
                      rd=[kst(5)], wr=[kst(6)])
                em.op("v", lambda v: v.tensor_scalar(
                    st[:, c0 + 7:c0 + 8], st[:, c0 + 2:c0 + 3],
                    st[:, c0 + 6:c0 + 7], -1.0, ALU.mult, ALU.mult),
                    rd=[kst(2), kst(6)], wr=[kst(7)])
                em.op("s", lambda s2: s2.activation(
                    xnscr[:, slot * D:(slot + 1) * D], src_fn(), AF.Identity,
                    bias=st[:, c0 + 7:c0 + 8], scale=st[:, c0 + 6:c0 + 7]),
                    rd=src_keys + [kst(6), kst(7)], wr=[(xnscr_name, slot)])

            def emit_ln_back(slot, xnscr, xnscr_name, psTR, psTR_name,
                             dstT, dst_name, tix):
                for hc in range(HC):
                    em.op("t", lambda t, hc_=hc: t.transpose(
                        psTR[slot][:, hc_ * 128:(hc_ + 1) * 128],
                        xnscr[:, slot * D + hc_ * 128: slot * D + (hc_ + 1) * 128],
                        ident_bf[:, :]),
                        rd=[(xnscr_name, slot), ("identbf", None)],
                        wr=[(psTR_name, (slot, hc))])
                em.op("v", lambda v: v.tensor_copy(
                    dstT[:, :].rearrange("p (c t) -> p c t", c=HC)[:, :, tix * 128:(tix + 1) * 128],
                    psTR[slot][:, :].rearrange("p (c q2) -> p c q2", c=HC)),
                    rd=[(psTR_name, (slot, None))], wr=[(dst_name, tix)])

            with ExitStack() as lctx:
                lsb = lambda n, s_, dt: lctx.enter_context(nc.sbuf_tensor(n, s_, dt))
                xst = lsb("xst", [128, 2 * D], F32)
                junk1 = lsb("junk1", [128, D], BF16)
                xnscr1 = lsb("xnscr1", [128, 2 * D], BF16)
                st1 = lsb("st1", [128, 16], F32)
                psTR = [lctx.enter_context(nc.psum_tensor(f"psTR{j}", [128, 1024], BF16))
                        for j in range(2)]
                jobs = []
                for t_ix in range(NT_SEQ):
                    jobs.append(("seq", t_ix))
                for i in range(NT):
                    jobs.append(("own", i))
                for n_, (kind, tix) in enumerate(jobs):
                    slot = n_ % 2
                    if kind == "seq":
                        em.op("q", lambda q, t_=tix, sl_=slot: q.dma_start(
                            out=xst[:, sl_ * D:(sl_ + 1) * D],
                            in_=din["x_seq"][t_ * 128:(t_ + 1) * 128, :]),
                            wr=[("xst", slot)], dma=True)
                        emit_ln_front(lambda sl_=slot: xst[:, sl_ * D:(sl_ + 1) * D],
                                      [("xst", slot)], st1, "st1", slot,
                                      xnscr1, "xnscr1", junk1, "junk1")
                    else:
                        emit_ln_front(lambda i_=tix: x2[:, i_ * D:(i_ + 1) * D],
                                      [("x2", (tix, None))], st1, "st1", slot,
                                      xnscr1, "xnscr1", junk1, "junk1")
                    if n_ > 0:
                        pkind, ptix = jobs[n_ - 1]
                        emit_ln_back((n_ - 1) % 2, xnscr1, "xnscr1", psTR, "psTR",
                                     xn1T if pkind == "seq" else xn1To,
                                     "xn1T" if pkind == "seq" else "xn1To", ptix)
                pkind, ptix = jobs[-1]
                emit_ln_back((len(jobs) - 1) % 2, xnscr1, "xnscr1", psTR, "psTR",
                             xn1T if pkind == "seq" else xn1To,
                             "xn1T" if pkind == "seq" else "xn1To", ptix)
            em.barrier()

            wq_sl = asb("wq_sl", [128, HC * 128], BF16)
            wk_sl = asb("wk_sl", [128, HC * 128], BF16)
            wv_sl = asb("wv_sl", [128, HC * 128], BF16)
            qT = asb("qT", [128, NT * 128], BF16)
            kM = asb("kM", [128, S], BF16)
            kts = asb("kts", [128, 512], BF16)
            vS = asb("vS", [128, S], BF16)
            p_sb = [asb(f"p_sb{j}", [128, 1024], BF16) for j in range(2)]
            pt_sb = [asb(f"pt_sb{j}", [128, 1024], BF16) for j in range(2)]
            acc = [asb(f"acc{j}", [128, 128], F32) for j in range(2)]
            nb_sb = asb("nb_sb", [128, 8], F32)
            l_sb = asb("l_sb", [128, 8], F32)
            rs_sb = asb("rs_sb", [128, 8], F32)
            cr_sb = asb("cr_sb", [128, 8], F32)
            rm_sb = asb("rm_sb", [128, 8], F32)
            rl_sb = asb("rl_sb", [128, 4], F32)

            psS = [actx.enter_context(nc.psum_tensor(f"psS{j}", [128, 1024], F32))
                   for j in range(2)]
            psPT = [actx.enter_context(nc.psum_tensor(f"psPT{j}", [128, 1024], BF16))
                    for j in range(2)]
            psAV = [actx.enter_context(nc.psum_tensor(f"psAV{j}", [128, 128], F32))
                    for j in range(2)]
            xn1T_v = xn1T[:, :].rearrange("p (c t) -> p c t", c=HC)
            xn1To_v = xn1To[:, :].rearrange("p (c t) -> p c t", c=HC)

            for hp in range(NH // 2):
                for nm, buf, off in (("wq", wq_sl, 0), ("wk", wk_sl, D), ("wv", wv_sl, 2 * D)):
                    em.op("q", lambda q, b_=buf, o_=off, hp_=hp: q.dma_start(
                        out=b_[:, :].rearrange("p (c f) -> p c f", c=HC),
                        in_=scr["qkv"].rearrange("(c p) f -> p c f", p=128)[
                            :, :, o_ + hp_ * 128: o_ + (hp_ + 1) * 128]),
                        rd=[("scr_qkv", None)], wr=[(nm, None)], dma=True)

                for tb in range(4):
                    j = tb % 2
                    for h2 in range(2):
                        for hc in range(HC):
                            em.op("t", lambda t, hc_=hc, tb_=tb, j_=j, h2_=h2: t.matmul(
                                psS[j_][h2_ * 64: h2_ * 64 + 64, h2_ * 512:(h2_ + 1) * 512],
                                wq_sl[:, hc_ * 128 + h2_ * 64: hc_ * 128 + h2_ * 64 + 64],
                                xn1To_v[:, hc_, tb_ * 512:(tb_ + 1) * 512],
                                start=(hc_ == 0), stop=(hc_ == HC - 1),
                                tile_position=(0, h2_ * 64)),
                                rd=[("wq", None), ("xn1To", None)], wr=[("psS", (j, h2))],
                                wmode=("check" if hc == 0 else ("record" if hc == HC - 1 else "skip")))
                        em.op("s", lambda s2, tb_=tb, j_=j, h2_=h2: s2.activation(
                            qT[h2_ * 64: h2_ * 64 + 64, tb_ * 512:(tb_ + 1) * 512],
                            psS[j_][h2_ * 64: h2_ * 64 + 64, h2_ * 512:(h2_ + 1) * 512],
                            AF.Copy, scale=0.125),
                            rd=[("psS", (j, h2))], wr=[("qT", tb)])
                for tb in range(8):
                    j = tb % 2
                    for hc in range(HC):
                        em.op("t", lambda t, hc_=hc, tb_=tb, j_=j: t.matmul(
                            psS[j_][:, 512:1024],
                            wk_sl[:, hc_ * 128:(hc_ + 1) * 128],
                            xn1T_v[:, hc_, tb_ * 512:(tb_ + 1) * 512],
                            start=(hc_ == 0), stop=(hc_ == HC - 1)),
                            rd=[("wk", None), ("xn1T", None)], wr=[("psS", (j, 1))],
                            wmode=("check" if hc == 0 else ("record" if hc == HC - 1 else "skip")))
                    em.op("v", lambda v, tb_=tb, j_=j: v.tensor_copy(
                        kts[:, :], psS[j_][:, 512:1024]),
                        rd=[("psS", (j, 1))], wr=[("kts", None)])
                    for jl in range(8):
                        for h2 in range(2):
                            em.op("t", lambda t, j_=j, jl_=jl, h2_=h2: t.transpose(
                                psPT[j_][h2_ * 64: h2_ * 64 + 64, jl_ * 64:(jl_ + 1) * 64],
                                kts[h2_ * 64: h2_ * 64 + 64, jl_ * 64:(jl_ + 1) * 64],
                                ident_bf[h2_ * 64: h2_ * 64 + 64, h2_ * 64: h2_ * 64 + 64],
                                tile_position=(h2_ * 64, h2_ * 64)),
                                rd=[("kts", None), ("identbf", None)],
                                wr=[("psPT", (j, jl, h2))])
                    em.op("v", lambda v, tb_=tb, j_=j: v.tensor_copy(
                        kM[:, :].rearrange("p (d jj) -> p d jj", d=64)[
                            :, :, tb_ * 8:(tb_ + 1) * 8],
                        psPT[j_][:, 0:512].rearrange("p (jj d) -> p d jj", jj=8)),
                        rd=[("psPT", (j, None, None))], wr=[("kM", tb)])
                for kc in range(S // 128):
                    j = kc % 2
                    for hc in range(HC):
                        em.op("t", lambda t, hc_=hc, kc_=kc, j_=j: t.matmul(
                            psAV[j_][:, :],
                            xn1T_v[:, hc_, kc_ * 128:(kc_ + 1) * 128],
                            wv_sl[:, hc_ * 128:(hc_ + 1) * 128],
                            start=(hc_ == 0), stop=(hc_ == HC - 1)),
                            rd=[("wv", None), ("xn1T", None)], wr=[("psAV", (j,))],
                            wmode=("check" if hc == 0 else ("record" if hc == HC - 1 else "skip")))
                    em.op("v", lambda v, kc_=kc, j_=j: v.tensor_copy(
                        vS[:, kc_ * 128:(kc_ + 1) * 128], psAV[j_][:, :]),
                        rd=[("psAV", (j,))], wr=[("vS", kc)])

                def emit_S(i, k, idx):
                    j = idx % 2
                    k0 = 512 * k
                    for h2 in range(2):
                        em.op("t", lambda t, k0_=k0, j_=j, i_=i, h2_=h2: t.matmul(
                            psS[j_][:, h2_ * 512: h2_ * 512 + 512],
                            qT[h2_ * 64: h2_ * 64 + 64, i_ * 128:(i_ + 1) * 128],
                            kM[h2_ * 64: h2_ * 64 + 64, k0_:k0_ + 512],
                            start=True, stop=True, tile_position=(h2_ * 64, 0)),
                            rd=[("qT", i // 4), ("kM", None)],
                            wr=[("psS", (j, h2))])

                def front(i, k, idx):
                    nblk = (i + 2) // 2
                    ai = i % 2
                    j = idx % 2
                    rk = idx % 4
                    pk = (idx - 1) % 4
                    c0 = (idx % 4) * 2
                    p0 = ((idx - 1) % 4) * 2
                    em.op("v", lambda v, j_=j, c0_=c0: v.reduce_max(
                        rm_sb[:, c0_:c0_ + 2],
                        psS[j_][:, :].rearrange("p (c f) -> p c f", c=2),
                        axis=AX.X, negate=True),
                        rd=[("psS", (j, 0)), ("psS", (j, 1))], wr=[("rm", rk)])
                    if k == nblk - 1:
                        u = i % 2
                        for h2 in range(2):
                            em.op("v", lambda v, j_=j, h2_=h2, u_=u: v.tensor_tensor(
                                psS[j_][:, h2_ * 512: h2_ * 512 + 512],
                                psS[j_][:, h2_ * 512: h2_ * 512 + 512],
                                mask_sb[:, u_ * 512:(u_ + 1) * 512], ALU.add),
                                rd=[("psS", (j, h2)), ("mask", None)],
                                wr=[("psS", (j, h2))])
                    if k == 0:
                        em.op("v", lambda v, c0_=c0: v.tensor_copy(
                            nb_sb[:, c0_:c0_ + 2], rm_sb[:, c0_:c0_ + 2]),
                            rd=[("rm", rk)], wr=[("nb", rk)])
                    else:
                        em.op("v", lambda v, c0_=c0, p0_=p0: v.tensor_tensor(
                            nb_sb[:, c0_:c0_ + 2], nb_sb[:, p0_:p0_ + 2],
                            rm_sb[:, c0_:c0_ + 2], ALU.min),
                            rd=[("rm", rk), ("nb", pk)], wr=[("nb", rk)])
                        em.op("v", lambda v, c0_=c0, p0_=p0: v.tensor_tensor(
                            cr_sb[:, c0_:c0_ + 2], nb_sb[:, c0_:c0_ + 2],
                            nb_sb[:, p0_:p0_ + 2], ALU.subtract),
                            rd=[("nb", rk), ("nb", pk)], wr=[("cr", rk)])
                        em.op("s", lambda s2, c0_=c0: s2.activation(
                            cr_sb[:, c0_:c0_ + 2], cr_sb[:, c0_:c0_ + 2], AF.Exp),
                            rd=[("cr", rk)], wr=[("cr", rk)])
                    for h2 in range(2):
                        em.op("s", lambda s2, j_=j, h2_=h2, c0_=c0: s2.activation(
                            p_sb[j_][:, h2_ * 512:(h2_ + 1) * 512],
                            psS[j_][:, h2_ * 512:(h2_ + 1) * 512], AF.Exp,
                            bias=nb_sb[:, c0_ + h2_: c0_ + h2_ + 1],
                            accum_out=rs_sb[:, c0_ + h2_: c0_ + h2_ + 1]),
                            rd=[("psS", (j, h2)), ("nb", rk)],
                            wr=[("p", (j, h2)), ("rs", (rk, h2))])

                def back(i, k, idx):
                    ai = i % 2
                    j = idx % 2
                    rk = idx % 4
                    pk = (idx - 1) % 4
                    c0 = (idx % 4) * 2
                    p0 = ((idx - 1) % 4) * 2
                    k0 = 512 * k
                    for h2 in range(2):
                        if k == 0:
                            em.op("v", lambda v, c0_=c0, h2_=h2: v.tensor_copy(
                                l_sb[:, c0_ + h2_:c0_ + h2_ + 1],
                                rs_sb[:, c0_ + h2_:c0_ + h2_ + 1]),
                                rd=[("rs", (rk, h2))], wr=[("l", (rk, h2))])
                        else:
                            em.op("v", lambda v, c0_=c0, p0_=p0, h2_=h2: v.scalar_tensor_tensor(
                                l_sb[:, c0_ + h2_:c0_ + h2_ + 1],
                                l_sb[:, p0_ + h2_:p0_ + h2_ + 1],
                                cr_sb[:, c0_ + h2_:c0_ + h2_ + 1],
                                rs_sb[:, c0_ + h2_:c0_ + h2_ + 1],
                                ALU.mult, ALU.add),
                                rd=[("l", (pk, h2)), ("cr", rk), ("rs", (rk, h2))],
                                wr=[("l", (rk, h2))])
                    for h2 in range(2):
                        for c in range(4):
                            em.op("t", lambda t, j_=j, h2_=h2, c_=c: t.transpose(
                                psPT[j_][:, h2_ * 512 + c_ * 128: h2_ * 512 + (c_ + 1) * 128],
                                p_sb[j_][:, h2_ * 512 + c_ * 128: h2_ * 512 + (c_ + 1) * 128],
                                ident_bf[:, :]),
                                rd=[("p", (j, h2)), ("identbf", None)],
                                wr=[("psPT", (j, h2, c))])
                    em.op("v", lambda v, j_=j: v.tensor_copy(
                        pt_sb[j_][:, :], psPT[j_][:, :]),
                        rd=[("psPT", (j, None, None))], wr=[("pt", (j, None))])
                    first = True
                    for c in range(4):
                        for h2, (lo, hi) in enumerate(((0, 64), (64, 128))):
                            last = (c == 3) and (h2 == 1)
                            em.op("t", lambda t, j_=j, c_=c, k0_=k0, lo_=lo, hi_=hi, h2_=h2,
                                  st_=first, sp_=last: t.matmul(
                                psAV[j_][:, lo_:hi_],
                                pt_sb[j_][:, h2_ * 512 + c_ * 128: h2_ * 512 + (c_ + 1) * 128],
                                vS[:, (k0_ // 128 + c_) * 128 + lo_:(k0_ // 128 + c_) * 128 + hi_],
                                start=st_, stop=sp_),
                                rd=[("pt", (j, h2)), ("vS", None)], wr=[("psAV", (j,))],
                                wmode=("check" if first else ("record" if last else "skip")))
                            first = False
                    if k == 0:
                        em.op("v", lambda v, j_=j, ai_=ai: v.tensor_copy(
                            acc[ai_][:, :], psAV[j_][:, :]),
                            rd=[("psAV", (j,))], wr=[("acc", ai)])
                    else:
                        for h2, (lo, hi) in enumerate(((0, 64), (64, 128))):
                            em.op("v", lambda v, j_=j, ai_=ai, lo_=lo, hi_=hi, h2_=h2, c0_=c0: v.scalar_tensor_tensor(
                                acc[ai_][:, lo_:hi_], acc[ai_][:, lo_:hi_],
                                cr_sb[:, c0_ + h2_:c0_ + h2_ + 1], psAV[j_][:, lo_:hi_],
                                ALU.mult, ALU.add),
                                rd=[("acc", ai), ("cr", rk), ("psAV", (j,))],
                                wr=[("acc", ai)])

                def finalize(i, lidx):
                    ai = i % 2
                    lc0 = (lidx % 4) * 2
                    lf = lidx % 4
                    em.op("v", lambda v, lc0_=lc0, ai_=ai: v.reciprocal(
                        rl_sb[:, ai_ * 2: ai_ * 2 + 2], l_sb[:, lc0_:lc0_ + 2]),
                        rd=[("l", (lf, 0)), ("l", (lf, 1))], wr=[("rl", ai)])
                    for h2, (lo, hi) in enumerate(((0, 64), (64, 128))):
                        em.op("v", lambda v, i_=i, hp_=hp, lo_=lo, hi_=hi, h2_=h2, ai_=ai: v.scalar_tensor_tensor(
                            x2[:, i_ * D + hp_ * 128 + lo_: i_ * D + hp_ * 128 + hi_],
                            acc[ai_][:, lo_:hi_], rl_sb[:, ai_ * 2 + h2_: ai_ * 2 + h2_ + 1],
                            x2[:, i_ * D + hp_ * 128 + lo_: i_ * D + hp_ * 128 + hi_],
                            ALU.mult, ALU.add),
                            rd=[("acc", ai), ("rl", ai), ("x2", (i, hp))],
                            wr=[("x2", (i, hp))])

                blist = [(i, k) for i in range(NT) for k in range((i + 2) // 2)]
                emit_S(*blist[0], 0)
                emit_S(*blist[1], 1)
                for idx, (i, k) in enumerate(blist):
                    front(i, k, idx)
                    if idx > 0:
                        pi, pk2 = blist[idx - 1]
                        back(pi, pk2, idx - 1)
                        if pk2 == (pi + 2) // 2 - 1:
                            finalize(pi, idx - 1)
                    if idx + 2 < len(blist):
                        emit_S(*blist[idx + 2], idx + 2)
                back(*blist[-1], len(blist) - 1)
                finalize(blist[-1][0], len(blist) - 1)
        em.barrier()
        if _DEBUG:
            em.op("q", lambda q: q.dma_start(
                out=dbg_x2a.rearrange("a p d -> p a d"),
                in_=x2[:, :].rearrange("p (a d) -> p a d", a=NT)),
                rd=[("x2", None)], wr=[("dbgx2a", None)], dma=True)

        with ExitStack() as fctx:
            fsb = lambda n, s_, dt: fctx.enter_context(nc.sbuf_tensor(n, s_, dt))
            xn2T = fsb("xn2T", [128, HC * NT * 128], BF16)
            xn2T_v = xn2T[:, :].rearrange("p (c t) -> p c t", c=HC)

            with ExitStack() as l2ctx:
                l2sb = lambda n, s_, dt: l2ctx.enter_context(nc.sbuf_tensor(n, s_, dt))
                junk2 = l2sb("junk2", [128, D], BF16)
                xnscr2 = l2sb("xnscr2", [128, 2 * D], BF16)
                st2 = l2sb("st2", [128, 16], F32)
                lgt = l2sb("lgt", [128, 2 * 8], F32)
                eqt = l2sb("eqt", [128, 2 * 8], F32)
                mm2 = l2sb("mm2", [128, 2 * 4], F32)
                bst = l2sb("bst", [16, 128], F32)
                psT2 = [l2ctx.enter_context(nc.psum_tensor(f"psT2{j}", [128, 1024], BF16))
                        for j in range(2)]
                psG = [l2ctx.enter_context(nc.psum_tensor(f"psG{j}", [128, 8], F32))
                       for j in range(2)]
                psM = l2ctx.enter_context(nc.psum_tensor("psM", [128, 16], F32))

                for nm, dst in (("b1", b1c), ("b2", b2c), ("bg", bgc)):
                    for e in range(E):
                        em.op("q", lambda q, nm_=nm, e_=e: q.dma_start(
                            out=bst[:, :], in_=din[nm_][e_].rearrange("(c f) -> c f", f=128)),
                            wr=[("bst", None)], dma=True)
                        em.op("t", lambda t: t.transpose(
                            psM[:, :], bst[:, :], ident_f[0:16, 0:16]),
                            rd=[("bst", None), ("identf", None)], wr=[("psM", None)])
                        em.op("v", lambda v, d_=dst, e_=e: v.tensor_copy(
                            d_[:, e_ * 16:(e_ + 1) * 16], psM[:, :]),
                            rd=[("psM", None)], wr=[(nm + "c", e)])

                def gate_tile(i, slot):
                    for hc in range(HC):
                        em.op("t", lambda t, hc_=hc, sl_=slot, i_=i: t.matmul(
                            psG[sl_][:, :],
                            xn2T_v[:, hc_, i_ * 128:(i_ + 1) * 128],
                            gw_sb[:, hc_ * 8:(hc_ + 1) * 8],
                            start=(hc_ == 0), stop=False),
                            rd=[("xn2T", i), ("gw", None)], wr=[("psG", (slot,))],
                            wmode=("check" if hc == 0 else "skip"))
                    em.op("t", lambda t, sl_=slot: t.matmul(
                        psG[sl_][:, :], ones_sb[0:1, 0:128], gb_sb[0:1, :],
                        start=False, stop=True),
                        rd=[("ones", None), ("gb", None)], wr=[("psG", (slot,))],
                        wmode="record")
                    g0, m0 = slot * 8, slot * 4
                    em.op("v", lambda v, sl_=slot, g0_=g0: v.tensor_copy(
                        lgt[:, g0_:g0_ + 8], psG[sl_][:, :]),
                        rd=[("psG", (slot,))], wr=[("lgt", (slot,))])
                    em.op("v", lambda v, g0_=g0, m0_=m0: v.reduce_max(
                        mm2[:, m0_:m0_ + 1], lgt[:, g0_:g0_ + 8], axis=AX.X),
                        rd=[("lgt", (slot,))], wr=[("mm2", (slot, 0))])
                    em.op("v", lambda v, g0_=g0, m0_=m0: v.tensor_scalar(
                        eqt[:, g0_:g0_ + 8], lgt[:, g0_:g0_ + 8],
                        mm2[:, m0_:m0_ + 1], None, ALU.is_equal),
                        rd=[("lgt", (slot,)), ("mm2", (slot, 0))], wr=[("eqt", (slot, 0))])
                    em.op("v", lambda v, g0_=g0: v.scalar_tensor_tensor(
                        lgt[:, g0_:g0_ + 8], eqt[:, g0_:g0_ + 8], -1.0e30,
                        lgt[:, g0_:g0_ + 8], ALU.mult, ALU.add),
                        rd=[("eqt", (slot, 0)), ("lgt", (slot,))], wr=[("lgt", (slot,))])
                    em.op("v", lambda v, g0_=g0, m0_=m0: v.reduce_max(
                        mm2[:, m0_ + 1:m0_ + 2], lgt[:, g0_:g0_ + 8], axis=AX.X),
                        rd=[("lgt", (slot,))], wr=[("mm2", (slot, 1))])
                    em.op("v", lambda v, m0_=m0: v.tensor_tensor(
                        mm2[:, m0_ + 2:m0_ + 3], mm2[:, m0_:m0_ + 1],
                        mm2[:, m0_ + 1:m0_ + 2], ALU.subtract),
                        rd=[("mm2", (slot, 0)), ("mm2", (slot, 1))], wr=[("mm2", (slot, 2))])
                    em.op("v", lambda v, m0_=m0: v.tensor_scalar(
                        mm2[:, m0_ + 3:m0_ + 4], mm2[:, m0_ + 2:m0_ + 3], -1.0, None, ALU.mult),
                        rd=[("mm2", (slot, 2))], wr=[("mm2", (slot, 3))])
                    em.op("s", lambda s2, m0_=m0: s2.activation(
                        mm2[:, m0_ + 2:m0_ + 4], mm2[:, m0_ + 2:m0_ + 4], AF.Sigmoid),
                        rd=[("mm2", (slot, 2)), ("mm2", (slot, 3))],
                        wr=[("mm2", (slot, 2)), ("mm2", (slot, 3))])
                    em.op("v", lambda v, g0_=g0, m0_=m0: v.tensor_scalar(
                        eqt[:, g0_:g0_ + 8], eqt[:, g0_:g0_ + 8],
                        mm2[:, m0_ + 2:m0_ + 3], None, ALU.mult),
                        rd=[("eqt", (slot, 0)), ("mm2", (slot, 2))], wr=[("eqt", (slot, 0))])
                    em.op("v", lambda v, g0_=g0, m0_=m0: v.tensor_scalar(
                        lgt[:, g0_:g0_ + 8], lgt[:, g0_:g0_ + 8],
                        mm2[:, m0_ + 1:m0_ + 2], None, ALU.is_equal),
                        rd=[("lgt", (slot,)), ("mm2", (slot, 1))], wr=[("lgt", (slot,))])
                    em.op("v", lambda v, g0_=g0, m0_=m0, i_=i: v.scalar_tensor_tensor(
                        we_sb[:, i_ * 8:(i_ + 1) * 8], lgt[:, g0_:g0_ + 8],
                        mm2[:, m0_ + 3:m0_ + 4], eqt[:, g0_:g0_ + 8],
                        ALU.mult, ALU.add),
                        rd=[("lgt", (slot,)), ("mm2", (slot, 3)), ("eqt", (slot, 0))],
                        wr=[("we", i)])

                for i in range(NT):
                    slot = i % 2
                    emit_ln_front(lambda i_=i: x2[:, i_ * D:(i_ + 1) * D],
                                  [("x2", (i, None))], st2, "st2", slot,
                                  xnscr2, "xnscr2", junk2, "junk2")
                    if i > 0:
                        emit_ln_back((i - 1) % 2, xnscr2, "xnscr2", psT2, "psT2",
                                     xn2T, "xn2T", i - 1)
                        gate_tile(i - 1, (i - 1) % 2)
                emit_ln_back((NT - 1) % 2, xnscr2, "xnscr2", psT2, "psT2",
                             xn2T, "xn2T", NT - 1)
                gate_tile(NT - 1, (NT - 1) % 2)
            em.barrier()
            if _DEBUG:
                em.op("q", lambda q: q.dma_start(out=dbg_we, in_=we_sb[:, :]),
                      rd=[("we", None)], wr=[("dbgwe", None)], dma=True)

            if _DEBUG:
                em.op("q", lambda q: q.dma_start(out=dbg_xn2, in_=xn2T[:, :]),
                      rd=[("xn2T", None)], wr=[("dbgxn2", None)], dma=True)
            z_sb = fsb("z_sb", [128, FC * 1024], BF16)
            s_sb = fsb("s_sb", [128, GC * 1024], BF16)
            w1_sl = [fsb(f"w1sl{j}", [128, HC * 128], BF16) for j in range(2)]
            w2_sl = [fsb(f"w2sl{j}", [128, HC * 128], BF16) for j in range(2)]
            wg_sl = [fsb(f"wgsl{j}", [128, FC * 128], BF16) for j in range(2)]
            w3_sl = [fsb(f"w3sl{j}", [128, GC * 256], BF16) for j in range(2)]
            x2t = fsb("x2t", [128, 2 * 512], BF16)

            psA1 = [fctx.enter_context(nc.psum_tensor(f"psA1{j}", [128, 512], F32))
                    for j in range(2)]
            psA2 = [fctx.enter_context(nc.psum_tensor(f"psA2{j}", [128, 512], F32))
                    for j in range(2)]
            psB = [fctx.enter_context(nc.psum_tensor(f"psB{j}", [128, 512], F32))
                   for j in range(2)]
            psC = [fctx.enter_context(nc.psum_tensor(f"psC{j}", [128, 256], F32))
                   for j in range(2)]

            def load_w1w2(e, fc):
                j = fc % 2
                for nm, buf, sname in (("w1s", w1_sl[j], "w1"), ("w2s", w2_sl[j], "w2")):
                    em.op("q", lambda q, b_=buf, sn_=sname, e_=e, fc_=fc: q.dma_start(
                        out=b_[:, :].rearrange("p (c f) -> p c f", c=HC),
                        in_=scr[sn_][e_].rearrange("(c p) f -> p c f", p=128)[
                            :, :, fc_ * 128:(fc_ + 1) * 128]),
                        rd=[("scr_" + sname, e)], wr=[(nm, j)], dma=True)

            def load_wg(e, gc):
                j = gc % 2
                for hh in range(2):
                    em.op("q", lambda q, e_=e, gc_=gc, j_=j, h_=hh: q.dma_start(
                        out=wg_sl[j_][:, :].rearrange("p (c f) -> p c f", c=FC)[
                            :, h_ * 8:(h_ + 1) * 8, :],
                        in_=scr["wg"][e_].rearrange("(c p) f -> p c f", p=128)[
                            :, h_ * 8:(h_ + 1) * 8, gc_ * 128:(gc_ + 1) * 128]),
                        rd=[("scr_wg", e)], wr=[("wgs", (j, hh))], dma=True)

            def load_w3(e, qh):
                j = qh % 2
                for hh in range(2):
                    em.op("q", lambda q, e_=e, qh_=qh, j_=j, h_=hh: q.dma_start(
                        out=w3_sl[j_][:, :].rearrange("p (c f) -> p c f", c=GC)[
                            :, h_ * 8:(h_ + 1) * 8, :],
                        in_=scr["w3"][e_].rearrange("(c p) f -> p c f", p=128)[
                            :, h_ * 8:(h_ + 1) * 8, qh_ * 256:(qh_ + 1) * 256]),
                        rd=[("scr_w3", e)], wr=[("w3s", (j, hh))], dma=True)

            def load_b3(e, row):
                em.op("g", lambda g, e_=e, r_=row: g.dma_start(
                    out=b3rows[r_:r_ + 1, :], in_=din["b3"][e_:e_ + 1, :]),
                    wr=[("b3row", row)], dma=True)

            ROWS = (0, 32, 64)
            load_b3(0, ROWS[0])
            for half in range(2):
                for e in range(E):
                    it = half * E + e
                    row = ROWS[it % 3]
                    if it + 1 < 2 * E:
                        load_b3((it + 1) % E, ROWS[(it + 1) % 3])
                    load_w1w2(e, 0)
                    load_w1w2(e, 1)
                    for fc in range(FC):
                        j = fc % 2
                        for tb in range(2):
                            jj = (fc * 2 + tb) % 2
                            for psA, wsl, wkey in ((psA1, w1_sl, "w1s"), (psA2, w2_sl, "w2s")):
                                for hc in range(HC):
                                    em.op("t", lambda t, hc_=hc, j_=j, jj_=jj, tb_=tb, h_=half,
                                          ps_=psA, w_=wsl: t.matmul(
                                        ps_[jj_][:, :],
                                        w_[j_][:, hc_ * 128:(hc_ + 1) * 128],
                                        xn2T_v[:, hc_, h_ * 1024 + tb_ * 512: h_ * 1024 + (tb_ + 1) * 512],
                                        start=(hc_ == 0), stop=(hc_ == HC - 1)),
                                        rd=[(wkey, j), ("xn2T", None)],
                                        wr=[("psA1" if psA is psA1 else "psA2", (jj,))],
                                        wmode=("check" if hc == 0 else ("record" if hc == HC - 1 else "skip")))
                            em.op("s", lambda s2, jj_=jj, e_=e, fc_=fc: s2.activation(
                                x2t[:, jj_ * 512:(jj_ + 1) * 512], psA2[jj_][:, :], AF.Identity,
                                bias=b2c[:, e_ * FC + fc_: e_ * FC + fc_ + 1]),
                                rd=[("psA2", (jj,)), ("b2c", e)], wr=[("x2t", jj)])
                            em.op("v", lambda v, jj_=jj, e_=e, fc_=fc, tb_=tb: v.scalar_tensor_tensor(
                                z_sb[:, fc_ * 1024 + tb_ * 512: fc_ * 1024 + (tb_ + 1) * 512],
                                psA1[jj_][:, :], b1c[:, e_ * FC + fc_: e_ * FC + fc_ + 1],
                                x2t[:, jj_ * 512:(jj_ + 1) * 512], ALU.add, ALU.mult),
                                rd=[("psA1", (jj,)), ("b1c", e), ("x2t", jj)],
                                wr=[("z", (fc, tb))])
                        if fc + 2 < FC:
                            load_w1w2(e, fc + 2)
                        elif fc == FC - 2:
                            load_wg(e, 0)
                            load_wg(e, 1)
                    if _DEBUG and half == 0 and e == 0:
                        em.op("q", lambda q: q.dma_start(out=dbg_z, in_=z_sb[:, :]),
                              rd=[("z", None)], wr=[("dbgz", None)], dma=True)
                    for gc in range(GC):
                        j = gc % 2
                        for tb in range(2):
                            jj = (gc * 2 + tb) % 2
                            for fc in range(FC):
                                em.op("t", lambda t, fc_=fc, j_=j, jj_=jj, tb_=tb: t.matmul(
                                    psB[jj_][:, :],
                                    wg_sl[j_][:, fc_ * 128:(fc_ + 1) * 128],
                                    z_sb[:, fc_ * 1024 + tb_ * 512: fc_ * 1024 + (tb_ + 1) * 512],
                                    start=(fc_ == 0), stop=(fc_ == FC - 1)),
                                    rd=[("wgs", (j, fc // 8)), ("z", (fc, tb))],
                                    wr=[("psB", (jj,))],
                                    wmode=("check" if fc == 0 else ("record" if fc == FC - 1 else "skip")))
                            em.op("s", lambda s2, jj_=jj, e_=e, gc_=gc, tb_=tb: s2.activation(
                                s_sb[:, gc_ * 1024 + tb_ * 512: gc_ * 1024 + (tb_ + 1) * 512],
                                psB[jj_][:, :], AF.Silu,
                                bias=bgc[:, e_ * GC + gc_: e_ * GC + gc_ + 1]),
                                rd=[("psB", (jj,)), ("bgc", e)], wr=[("s", (gc, tb))])
                        if gc + 2 < GC:
                            load_wg(e, gc + 2)
                        elif gc == GC - 2:
                            load_w3(e, 0)
                            load_w3(e, 1)
                    if _DEBUG and half == 0 and e == 0:
                        em.op("q", lambda q: q.dma_start(out=dbg_s, in_=s_sb[:, :]),
                              rd=[("s", None)], wr=[("dbgs", None)], dma=True)
                    for qh in range(4):
                        j = qh % 2
                        for tt in range(8):
                            jj = (qh * 8 + tt) % 2
                            em.op("t", lambda t, jj_=jj, r_=row, qh_=qh: t.matmul(
                                psC[jj_][:, :], ones_sb[r_:r_ + 1, 0:128],
                                b3rows[r_:r_ + 1, qh_ * 256:(qh_ + 1) * 256],
                                start=True, stop=False),
                                rd=[("ones", None), ("b3row", row)], wr=[("psC", (jj,))],
                                wmode="check")
                            for gc in range(GC):
                                em.op("t", lambda t, gc_=gc, j_=j, jj_=jj, tt_=tt: t.matmul(
                                    psC[jj_][:, :],
                                    s_sb[:, gc_ * 1024 + tt_ * 128: gc_ * 1024 + (tt_ + 1) * 128],
                                    w3_sl[j_][:, gc_ * 256:(gc_ + 1) * 256],
                                    start=False, stop=(gc_ == GC - 1)),
                                    rd=[("s", (gc, tt // 4)), ("w3s", (j, gc // 8))],
                                    wr=[("psC", (jj,))],
                                    wmode=("record" if gc == GC - 1 else "skip"))
                            i_glob = half * 8 + tt
                            em.op("v", lambda v, jj_=jj, i_=i_glob, qh_=qh, e_=e: v.scalar_tensor_tensor(
                                x2[:, i_ * D + qh_ * 256: i_ * D + (qh_ + 1) * 256],
                                psC[jj_][:, :], we_sb[:, i_ * 8 + e_: i_ * 8 + e_ + 1],
                                x2[:, i_ * D + qh_ * 256: i_ * D + (qh_ + 1) * 256],
                                ALU.mult, ALU.add),
                                rd=[("psC", (jj,)), ("we", i_glob), ("x2", (i_glob, ("f", qh)))],
                                wr=[("x2", (i_glob, ("f", qh)))])
                        if qh + 2 < 4:
                            load_w3(e, qh + 2)
        em.barrier()

        if _DEBUG:
            em.op("q", lambda q: q.dma_start(
                out=dbg_x2e.rearrange("a p d -> p a d"),
                in_=x2[:, :].rearrange("p (a d) -> p a d", a=NT)),
                rd=[("x2", None)], wr=[("dbgx2e", None)], dma=True)
        em.op("q", lambda q: q.dma_start(
            out=out_hbm.rearrange("a p d -> p a d"),
            in_=x2[:, :].rearrange("p (a d) -> p a d", a=NT)),
            rd=[("x2", None)], wr=[("outhbm", None)], dma=True)
        em.wait_all("q")

        with nc.Block() as block:
            em.build(block)
    return nc


_CACHE = {}


def _get_program():
    if "nc" not in _CACHE:
        _CACHE["nc"] = build_program()
    return _CACHE["nc"]


def kernel(**inputs):
    x = np.asarray(inputs["L0_attn_x"], dtype=np.float32)
    wqkv = np.asarray(inputs["L0_attn_qkv_weights"], dtype=np.float32)
    gate_w = np.asarray(inputs["L0_ffn_gate_w"], dtype=np.float32)
    gate_b = np.asarray(inputs["L0_ffn_gate_b"], dtype=np.float32).reshape(1, E)
    feed = {
        "wqkv": wqkv, "gate_w": gate_w, "gate_b": gate_b,
        "w1": np.asarray(inputs["L0_ffn_w1"], dtype=np.float32),
        "b1": np.asarray(inputs["L0_ffn_b1"], dtype=np.float32),
        "w2": np.asarray(inputs["L0_ffn_w2"], dtype=np.float32),
        "b2": np.asarray(inputs["L0_ffn_b2"], dtype=np.float32),
        "wg": np.asarray(inputs["L0_ffn_wg"], dtype=np.float32),
        "bg": np.asarray(inputs["L0_ffn_bg"], dtype=np.float32),
        "w3": np.asarray(inputs["L0_ffn_w3"], dtype=np.float32),
        "b3": np.asarray(inputs["L0_ffn_b3"], dtype=np.float32),
        "ident_bf": np.eye(128, dtype=ml_dtypes.bfloat16),
        "ident_f": np.eye(128, dtype=np.float32),
    }
    masks = []
    for p in range(2):
        cols = np.arange(256)[None, :]
        rows = np.arange(128)[:, None]
        tri = np.where(cols <= 128 * p + rows, 0.0, NEG).astype(np.float32)
        neg = np.full((128, 256), NEG, np.float32)
        zer = np.zeros((128, 256), np.float32)
        m0 = np.concatenate([tri, neg], axis=1)
        m1 = np.concatenate([zer, tri], axis=1)
        masks.append(np.concatenate([m0, m1], axis=1))

    nc = _get_program()
    in_maps = []
    for c in range(8):
        b, p = c >> 1, c & 1
        m = dict(feed)
        m["x_seq"] = np.ascontiguousarray(x[b])
        m["x_own"] = np.ascontiguousarray(x[b].reshape(NT, 2, 128, D)[:, p])
        m["mask"] = masks[p]
        in_maps.append(m)
    res = run_bass_kernel_spmd(nc, in_maps, core_ids=list(range(8)))
    out = np.empty((B, S, D), dtype=np.float32)
    ov = out.reshape(B, NT, 2, 128, D)
    for c in range(8):
        ov[c >> 1, :, c & 1] = res.results[c]["out"]
    return out


# version 13
# speedup vs baseline: 1.0378x; 1.0378x over previous
import numpy as np
import ml_dtypes
from contextlib import ExitStack

import concourse.bass as bass
import concourse.mybir as mybir
from concourse.bass_utils import run_bass_kernel_spmd

F32 = mybir.dt.float32
BF16 = mybir.dt.bfloat16
AF = mybir.ActivationFunctionType
ALU = mybir.AluOpType
AX = mybir.AxisListType

B, S, D = 4, 4096, 1024
NH, DH = 16, 64
E, DFF = 8, 2048
NT = 16
NT_SEQ = 32
HC = 8
FC = DFF // 128
GC = DFF // 128
NEG = -1.0e9
_DEBUG = False


def _sub_overlap(a, b):
    if a is None or b is None:
        return True
    if not isinstance(a, tuple):
        a = (a,)
    if not isinstance(b, tuple):
        b = (b,)
    for x, y in zip(a, b):
        if x is None or y is None:
            continue
        if x != y:
            return False
    return True


class Em:
    ENGS = ("t", "v", "s", "g", "q")

    NDMA = {"q": 8, "g": 1}

    def __init__(self, nc):
        self.nc = nc
        self.qs = {e: [] for e in self.ENGS}
        self.sem_names = (["t", "v", "s", "g", "q"]
                          + [f"dq{i}" for i in range(self.NDMA["q"])]
                          + [f"dg{i}" for i in range(self.NDMA["g"])])
        self.dma_n = {"q": 0, "g": 0}
        self.counts = {n: 0 for n in self.sem_names}
        self.wm = {e: {} for e in self.ENGS}
        self.W = {}
        self.R = {}
        self._sems = None

    def _edges_read(self, key):
        name, sub = key
        out = []
        for s2, ev in self.W.get(name, {}).items():
            if _sub_overlap(sub, s2):
                out.append(ev)
        return out

    def _edges_write(self, key):
        name, sub = key
        out = list(self._edges_read(key))
        for s2, evs in self.R.get(name, {}).items():
            if _sub_overlap(sub, s2):
                out.extend(evs)
        return out

    def _note_read(self, key, ev):
        name, sub = key
        self.R.setdefault(name, {}).setdefault(sub, []).append(ev)

    def _note_write(self, key, ev):
        name, sub = key
        self.W.setdefault(name, {})[sub] = ev
        rd = self.R.get(name)
        if rd:
            dead = [s2 for s2 in rd if _sub_overlap(sub, s2)]
            for s2 in dead:
                del rd[s2]

    def op(self, eng, fn, rd=(), wr=(), dma=False, wmode="full"):
        need = []
        for k in rd:
            need.extend(self._edges_read(k))
        if wmode in ("full", "check"):
            for k in wr:
                need.extend(self._edges_write(k))
        waits = []
        wm = self.wm[eng]
        for sem, val, clock in need:
            if wm.get(sem, 0) < val:
                waits.append((sem, val))
                for s2, v2 in clock.items():
                    if wm.get(s2, 0) < v2:
                        wm[s2] = v2
                wm[sem] = max(wm.get(sem, 0), val)
        if dma:
            pref = {"q": "dq", "g": "dg"}[eng]
            n = self.dma_n[eng]
            self.dma_n[eng] = n + 1
            sem = f"{pref}{n % self.NDMA[eng]}"
            inc = 16
            prev = self.counts[sem]
            if wm.get(sem, 0) < prev:
                waits.append((sem, prev))
                wm[sem] = prev
        else:
            sem, inc = eng, 1
        self.counts[sem] += inc
        ev = (sem, self.counts[sem], dict(wm, **{sem: self.counts[sem]}))
        sems_ref = lambda: self._sems

        def closure(epx, _w=tuple(waits), _fn=fn, _sem=sem, _inc=inc, _sr=sems_ref):
            sems = _sr()
            for s2, v2 in _w:
                epx.wait_ge(sems[s2], v2)
            _fn(epx).then_inc(sems[_sem], _inc)

        self.qs[eng].append(closure)
        for k in rd:
            self._note_read(k, ev)
        if wmode in ("full", "record"):
            for k in wr:
                self._note_write(k, ev)
        return ev

    def wait_all(self, eng):
        snap = dict(self.counts)
        wm = self.wm[eng]
        waits = [(s2, v2) for s2, v2 in snap.items() if v2 > wm.get(s2, 0)]
        for s2, v2 in waits:
            wm[s2] = v2
        sems_ref = lambda: self._sems

        def closure(epx, _w=tuple(waits), _sr=sems_ref):
            sems = _sr()
            for s2, v2 in _w:
                epx.wait_ge(sems[s2], v2)

        self.qs[eng].append(closure)

    def barrier(self):
        for e in self.ENGS:
            self.wait_all(e)

    def build(self, block):
        qs = self.qs
        block.sync(lambda e: [f(e) for f in qs["q"]])
        block.gpsimd(lambda e: [f(e) for f in qs["g"]])
        block.tensor(lambda e: [f(e) for f in qs["t"]])
        block.vector(lambda e: [f(e) for f in qs["v"]])
        block.scalar(lambda e: [f(e) for f in qs["s"]])


def build_program():
    nc = bass.Bass("TRN2", target_bir_lowering=False)

    din = {}
    din["x_seq"] = nc.dram_tensor("x_seq", [S, D], F32, kind="ExternalInput").ap()
    din["x_own"] = nc.dram_tensor("x_own", [NT, 128, D], F32, kind="ExternalInput").ap()
    din["wqkv"] = nc.dram_tensor("wqkv", [D, 3 * D], F32, kind="ExternalInput").ap()
    din["gate_w"] = nc.dram_tensor("gate_w", [D, E], F32, kind="ExternalInput").ap()
    din["gate_b"] = nc.dram_tensor("gate_b", [1, E], F32, kind="ExternalInput").ap()
    for nm, shp in (("w1", [E, D, DFF]), ("w2", [E, D, DFF]), ("wg", [E, DFF, DFF]),
                    ("w3", [E, DFF, D]), ("b1", [E, DFF]), ("b2", [E, DFF]),
                    ("bg", [E, DFF]), ("b3", [E, D])):
        din[nm] = nc.dram_tensor(nm, shp, F32, kind="ExternalInput").ap()
    din["mask"] = nc.dram_tensor("mask", [128, 1024], F32, kind="ExternalInput").ap()
    din["ident_bf"] = nc.dram_tensor("ident_bf", [128, 128], BF16, kind="ExternalInput").ap()
    din["ident_f"] = nc.dram_tensor("ident_f", [128, 128], F32, kind="ExternalInput").ap()
    out_hbm = nc.dram_tensor("out", [NT, 128, D], F32, kind="ExternalOutput").ap()
    if _DEBUG:
        dbg_x2a = nc.dram_tensor("dbg_x2a", [NT, 128, D], F32, kind="ExternalOutput").ap()
        dbg_we = nc.dram_tensor("dbg_we", [128, NT * E], F32, kind="ExternalOutput").ap()
        dbg_xn2 = nc.dram_tensor("dbg_xn2", [128, HC * NT * 128], BF16, kind="ExternalOutput").ap()
        dbg_z = nc.dram_tensor("dbg_z", [128, FC * 1024], BF16, kind="ExternalOutput").ap()
        dbg_s = nc.dram_tensor("dbg_s", [128, GC * 1024], BF16, kind="ExternalOutput").ap()
        dbg_x2e = nc.dram_tensor("dbg_x2e", [NT, 128, D], F32, kind="ExternalOutput").ap()

    scr = {}
    scr["qkv"] = nc.dram_tensor("scr_qkv", [D, 3 * D], BF16, kind="Internal").ap()
    scr["w1"] = nc.dram_tensor("scr_w1", [E, D, DFF], BF16, kind="Internal").ap()
    scr["w2"] = nc.dram_tensor("scr_w2", [E, D, DFF], BF16, kind="Internal").ap()
    scr["wg"] = nc.dram_tensor("scr_wg", [E, DFF, DFF], BF16, kind="Internal").ap()
    scr["w3"] = nc.dram_tensor("scr_w3", [E, DFF, D], BF16, kind="Internal").ap()

    em = Em(nc)

    with ExitStack() as ctx:
        sb = lambda n, s_, dt: ctx.enter_context(nc.sbuf_tensor(n, s_, dt))

        x2 = sb("x2", [128, NT * D], F32)
        ident_bf = sb("identbf_sb", [128, 128], BF16)
        ident_f = sb("identf_sb", [128, 128], F32)
        mask_sb = sb("mask_sb", [128, 1024], F32)
        ones_sb = sb("ones_sb", [65, 256], BF16)
        gw_sb = sb("gw_sb", [128, HC * E], BF16)
        gb_sb = sb("gb_sb", [1, E], BF16)
        we_sb = sb("we_sb", [128, NT * E], F32)
        b1c = sb("b1c", [128, E * FC], F32)
        b2c = sb("b2c", [128, E * FC], F32)
        bgc = sb("bgc", [128, E * GC], F32)
        b3rows = sb("b3rows", [65, D], BF16)

        sems = {}
        for n in em.sem_names:
            sems[n] = ctx.enter_context(nc.semaphore(n))
        em._sems = sems

        em.op("g", lambda g: g.dma_start(out=scr["qkv"], in_=din["wqkv"]),
              wr=[("scr_qkv", None)], dma=True)
        em.op("g", lambda g: g.dma_start(out=gb_sb[:, :], in_=din["gate_b"]),
              wr=[("gb", None)], dma=True)
        em.op("g", lambda g: g.dma_start(
            out=gw_sb[:, :].rearrange("p (c e) -> p c e", c=HC),
            in_=din["gate_w"].rearrange("(c p) e -> p c e", p=128)),
            wr=[("gw", None)], dma=True)
        em.op("g", lambda g: g.memset(ones_sb[:, :], 1.0), wr=[("ones", None)])
        for e in range(E):
            for nm in ("w1", "w2", "wg", "w3"):
                em.op("g", lambda g, nm_=nm, e_=e: g.dma_start(
                    out=scr[nm_][e_], in_=din[nm_][e_]),
                    wr=[("scr_" + nm, e)], dma=True)
        em.op("q", lambda q: q.dma_start(
            out=x2[:, :].rearrange("p (a d) -> p a d", a=NT),
            in_=din["x_own"].rearrange("a p d -> p a d")),
            wr=[("x2", None)], dma=True)
        em.op("q", lambda q: q.dma_start(out=ident_bf[:, :], in_=din["ident_bf"]),
              wr=[("identbf", None)], dma=True)
        em.op("q", lambda q: q.dma_start(out=ident_f[:, :], in_=din["ident_f"]),
              wr=[("identf", None)], dma=True)
        em.op("q", lambda q: q.dma_start(out=mask_sb[:, :], in_=din["mask"]),
              wr=[("mask", None)], dma=True)

        with ExitStack() as actx:
            asb = lambda n, s_, dt: actx.enter_context(nc.sbuf_tensor(n, s_, dt))
            xn1T = asb("xn1T", [128, HC * S], BF16)
            xn1To = asb("xn1To", [128, HC * NT * 128], BF16)

            def emit_ln(src_fn, src_keys, st, st_name, slot, xnscr, xnscr_name,
                        psTR, psTR_name, dstT, dst_name, tix, junk, junk_name):
                c0 = slot * 8
                kst = lambda n_: (st_name, (slot, n_))
                em.op("v", lambda v: v.reduce_sum(st[:, c0:c0 + 1], src_fn(), axis=AX.X),
                      rd=src_keys, wr=[kst(0)])
                em.op("s", lambda s2: s2.activation(
                    junk[:, :], src_fn(), AF.Square, accum_out=st[:, c0 + 1:c0 + 2]),
                    rd=src_keys, wr=[(junk_name, None), kst(1)])
                em.op("v", lambda v: v.tensor_scalar(
                    st[:, c0 + 2:c0 + 3], st[:, c0:c0 + 1], 1.0 / D, None, ALU.mult),
                    rd=[kst(0)], wr=[kst(2)])
                em.op("v", lambda v: v.tensor_scalar(
                    st[:, c0 + 3:c0 + 4], st[:, c0 + 1:c0 + 2], 1.0 / D, 1e-5,
                    ALU.mult, ALU.add),
                    rd=[kst(1)], wr=[kst(3)])
                em.op("v", lambda v: v.scalar_tensor_tensor(
                    st[:, c0 + 4:c0 + 5], st[:, c0 + 2:c0 + 3], st[:, c0 + 2:c0 + 3],
                    st[:, c0 + 3:c0 + 4], ALU.mult, ALU.subtract),
                    rd=[kst(2), kst(3)], wr=[kst(4)])
                em.op("s", lambda s2: s2.activation(
                    st[:, c0 + 5:c0 + 6], st[:, c0 + 4:c0 + 5], AF.Sqrt, scale=-1.0),
                    rd=[kst(4)], wr=[kst(5)])
                em.op("v", lambda v: v.reciprocal(st[:, c0 + 6:c0 + 7], st[:, c0 + 5:c0 + 6]),
                      rd=[kst(5)], wr=[kst(6)])
                em.op("v", lambda v: v.tensor_scalar(
                    st[:, c0 + 7:c0 + 8], st[:, c0 + 2:c0 + 3],
                    st[:, c0 + 6:c0 + 7], -1.0, ALU.mult, ALU.mult),
                    rd=[kst(2), kst(6)], wr=[kst(7)])
                em.op("s", lambda s2: s2.activation(
                    xnscr[:, slot * D:(slot + 1) * D], src_fn(), AF.Identity,
                    bias=st[:, c0 + 7:c0 + 8], scale=st[:, c0 + 6:c0 + 7]),
                    rd=src_keys + [kst(6), kst(7)], wr=[(xnscr_name, slot)])
                for hc in range(HC):
                    em.op("t", lambda t, hc_=hc: t.transpose(
                        psTR[slot][:, hc_ * 128:(hc_ + 1) * 128],
                        xnscr[:, slot * D + hc_ * 128: slot * D + (hc_ + 1) * 128],
                        ident_bf[:, :]),
                        rd=[(xnscr_name, slot), ("identbf", None)],
                        wr=[(psTR_name, (slot, hc))])
                em.op("v", lambda v: v.tensor_copy(
                    dstT[:, :].rearrange("p (c t) -> p c t", c=HC)[:, :, tix * 128:(tix + 1) * 128],
                    psTR[slot][:, :].rearrange("p (c q2) -> p c q2", c=HC)),
                    rd=[(psTR_name, (slot, None))], wr=[(dst_name, tix)])

            with ExitStack() as lctx:
                lsb = lambda n, s_, dt: lctx.enter_context(nc.sbuf_tensor(n, s_, dt))
                xst = lsb("xst", [128, 2 * D], F32)
                junk1 = lsb("junk1", [128, D], BF16)
                xnscr1 = lsb("xnscr1", [128, 2 * D], BF16)
                st1 = lsb("st1", [128, 16], F32)
                psTR = [lctx.enter_context(nc.psum_tensor(f"psTR{j}", [128, 1024], BF16))
                        for j in range(2)]
                for t_ix in range(NT_SEQ):
                    slot = t_ix % 2
                    em.op("q", lambda q, t_=t_ix, sl_=slot: q.dma_start(
                        out=xst[:, sl_ * D:(sl_ + 1) * D],
                        in_=din["x_seq"][t_ * 128:(t_ + 1) * 128, :]),
                        wr=[("xst", slot)], dma=True)
                    emit_ln(lambda sl_=slot: xst[:, sl_ * D:(sl_ + 1) * D],
                            [("xst", slot)], st1, "st1", slot, xnscr1, "xnscr1",
                            psTR, "psTR", xn1T, "xn1T", t_ix, junk1, "junk1")
                for i in range(NT):
                    slot = i % 2
                    emit_ln(lambda i_=i: x2[:, i_ * D:(i_ + 1) * D],
                            [("x2", (i, None))], st1, "st1", slot, xnscr1, "xnscr1",
                            psTR, "psTR", xn1To, "xn1To", i, junk1, "junk1")
            em.barrier()

            wq_sl = asb("wq_sl", [128, HC * 128], BF16)
            wk_sl = asb("wk_sl", [128, HC * 128], BF16)
            wv_sl = asb("wv_sl", [128, HC * 128], BF16)
            qT = asb("qT", [128, NT * 128], BF16)
            kM = asb("kM", [128, S], BF16)
            kts = asb("kts", [128, 2 * 512], BF16)
            vS = asb("vS", [128, S], BF16)
            p_sb = [asb(f"p_sb{j}", [128, 1024], BF16) for j in range(2)]
            pt_sb = [asb(f"pt_sb{j}", [128, 1024], BF16) for j in range(2)]
            acc = [asb(f"acc{j}", [128, 128], F32) for j in range(2)]
            nb_sb = asb("nb_sb", [128, 8], F32)
            l_sb = asb("l_sb", [128, 8], F32)
            rs_sb = asb("rs_sb", [128, 8], F32)
            cr_sb = asb("cr_sb", [128, 8], F32)
            rm_sb = asb("rm_sb", [128, 8], F32)
            rl_sb = asb("rl_sb", [128, 4], F32)

            psS = [actx.enter_context(nc.psum_tensor(f"psS{j}", [128, 1024], F32))
                   for j in range(2)]
            psPT = [actx.enter_context(nc.psum_tensor(f"psPT{j}", [128, 1024], BF16))
                    for j in range(2)]
            psAV = [actx.enter_context(nc.psum_tensor(f"psAV{j}", [128, 128], F32))
                    for j in range(2)]
            xn1T_v = xn1T[:, :].rearrange("p (c t) -> p c t", c=HC)
            xn1To_v = xn1To[:, :].rearrange("p (c t) -> p c t", c=HC)

            for hp in range(NH // 2):
                for nm, buf, off in (("wq", wq_sl, 0), ("wk", wk_sl, D), ("wv", wv_sl, 2 * D)):
                    em.op("q", lambda q, b_=buf, o_=off, hp_=hp: q.dma_start(
                        out=b_[:, :].rearrange("p (c f) -> p c f", c=HC),
                        in_=scr["qkv"].rearrange("(c p) f -> p c f", p=128)[
                            :, :, o_ + hp_ * 128: o_ + (hp_ + 1) * 128]),
                        rd=[("scr_qkv", None)], wr=[(nm, None)], dma=True)

                for tb in range(4):
                    j = tb % 2
                    for h2 in range(2):
                        for hc in range(HC):
                            em.op("t", lambda t, hc_=hc, tb_=tb, j_=j, h2_=h2: t.matmul(
                                psS[j_][h2_ * 64: h2_ * 64 + 64, h2_ * 512:(h2_ + 1) * 512],
                                wq_sl[:, hc_ * 128 + h2_ * 64: hc_ * 128 + h2_ * 64 + 64],
                                xn1To_v[:, hc_, tb_ * 512:(tb_ + 1) * 512],
                                start=(hc_ == 0), stop=(hc_ == HC - 1),
                                tile_position=(0, h2_ * 64)),
                                rd=[("wq", None), ("xn1To", None)], wr=[("psS", (j, h2))],
                                wmode=("check" if hc == 0 else ("record" if hc == HC - 1 else "skip")))
                        em.op("s", lambda s2, tb_=tb, j_=j, h2_=h2: s2.activation(
                            qT[h2_ * 64: h2_ * 64 + 64, tb_ * 512:(tb_ + 1) * 512],
                            psS[j_][h2_ * 64: h2_ * 64 + 64, h2_ * 512:(h2_ + 1) * 512],
                            AF.Copy, scale=0.125),
                            rd=[("psS", (j, h2))], wr=[("qT", tb)])
                def kfront(tb):
                    j = tb % 2
                    for hc in range(HC):
                        em.op("t", lambda t, hc_=hc, tb_=tb, j_=j: t.matmul(
                            psS[j_][:, 512:1024],
                            wk_sl[:, hc_ * 128:(hc_ + 1) * 128],
                            xn1T_v[:, hc_, tb_ * 512:(tb_ + 1) * 512],
                            start=(hc_ == 0), stop=(hc_ == HC - 1)),
                            rd=[("wk", None), ("xn1T", None)], wr=[("psS", (j, 1))],
                            wmode=("check" if hc == 0 else ("record" if hc == HC - 1 else "skip")))
                    em.op("v", lambda v, tb_=tb, j_=j: v.tensor_copy(
                        kts[:, j_ * 512:(j_ + 1) * 512], psS[j_][:, 512:1024]),
                        rd=[("psS", (j, 1))], wr=[("kts", j)])

                def kback(tb):
                    j = tb % 2
                    for jl in range(8):
                        for h2 in range(2):
                            em.op("t", lambda t, j_=j, jl_=jl, h2_=h2: t.transpose(
                                psPT[j_][h2_ * 64: h2_ * 64 + 64, jl_ * 64:(jl_ + 1) * 64],
                                kts[h2_ * 64: h2_ * 64 + 64, j_ * 512 + jl_ * 64: j_ * 512 + (jl_ + 1) * 64],
                                ident_bf[h2_ * 64: h2_ * 64 + 64, h2_ * 64: h2_ * 64 + 64],
                                tile_position=(h2_ * 64, h2_ * 64)),
                                rd=[("kts", j), ("identbf", None)],
                                wr=[("psPT", (j, jl, h2))])
                    em.op("v", lambda v, tb_=tb, j_=j: v.tensor_copy(
                        kM[:, :].rearrange("p (d jj) -> p d jj", d=64)[
                            :, :, tb_ * 8:(tb_ + 1) * 8],
                        psPT[j_][:, 0:512].rearrange("p (jj d) -> p d jj", jj=8)),
                        rd=[("psPT", (j, None, None))], wr=[("kM", tb)])

                for tb in range(8):
                    kfront(tb)
                    if tb > 0:
                        kback(tb - 1)
                kback(7)
                for kc in range(S // 128):
                    j = kc % 2
                    for hc in range(HC):
                        em.op("t", lambda t, hc_=hc, kc_=kc, j_=j: t.matmul(
                            psAV[j_][:, :],
                            xn1T_v[:, hc_, kc_ * 128:(kc_ + 1) * 128],
                            wv_sl[:, hc_ * 128:(hc_ + 1) * 128],
                            start=(hc_ == 0), stop=(hc_ == HC - 1)),
                            rd=[("wv", None), ("xn1T", None)], wr=[("psAV", (j,))],
                            wmode=("check" if hc == 0 else ("record" if hc == HC - 1 else "skip")))
                    em.op("v", lambda v, kc_=kc, j_=j: v.tensor_copy(
                        vS[:, kc_ * 128:(kc_ + 1) * 128], psAV[j_][:, :]),
                        rd=[("psAV", (j,))], wr=[("vS", kc)])

                def emit_S(i, k, idx):
                    j = idx % 2
                    k0 = 512 * k
                    for h2 in range(2):
                        em.op("t", lambda t, k0_=k0, j_=j, i_=i, h2_=h2: t.matmul(
                            psS[j_][:, h2_ * 512: h2_ * 512 + 512],
                            qT[h2_ * 64: h2_ * 64 + 64, i_ * 128:(i_ + 1) * 128],
                            kM[h2_ * 64: h2_ * 64 + 64, k0_:k0_ + 512],
                            start=True, stop=True, tile_position=(h2_ * 64, 0)),
                            rd=[("qT", i // 4), ("kM", None)],
                            wr=[("psS", (j, h2))])

                def front(i, k, idx):
                    nblk = (i + 2) // 2
                    ai = i % 2
                    j = idx % 2
                    rk = idx % 4
                    pk = (idx - 1) % 4
                    c0 = (idx % 4) * 2
                    p0 = ((idx - 1) % 4) * 2
                    em.op("v", lambda v, j_=j, c0_=c0: v.reduce_max(
                        rm_sb[:, c0_:c0_ + 2],
                        psS[j_][:, :].rearrange("p (c f) -> p c f", c=2),
                        axis=AX.X, negate=True),
                        rd=[("psS", (j, 0)), ("psS", (j, 1))], wr=[("rm", rk)])
                    if k == nblk - 1:
                        u = i % 2
                        for h2 in range(2):
                            em.op("v", lambda v, j_=j, h2_=h2, u_=u: v.tensor_tensor(
                                psS[j_][:, h2_ * 512: h2_ * 512 + 512],
                                psS[j_][:, h2_ * 512: h2_ * 512 + 512],
                                mask_sb[:, u_ * 512:(u_ + 1) * 512], ALU.add),
                                rd=[("psS", (j, h2)), ("mask", None)],
                                wr=[("psS", (j, h2))])
                    if k == 0:
                        em.op("v", lambda v, c0_=c0: v.tensor_copy(
                            nb_sb[:, c0_:c0_ + 2], rm_sb[:, c0_:c0_ + 2]),
                            rd=[("rm", rk)], wr=[("nb", rk)])
                    else:
                        em.op("v", lambda v, c0_=c0, p0_=p0: v.tensor_tensor(
                            nb_sb[:, c0_:c0_ + 2], nb_sb[:, p0_:p0_ + 2],
                            rm_sb[:, c0_:c0_ + 2], ALU.min),
                            rd=[("rm", rk), ("nb", pk)], wr=[("nb", rk)])
                        em.op("v", lambda v, c0_=c0, p0_=p0: v.tensor_tensor(
                            cr_sb[:, c0_:c0_ + 2], nb_sb[:, c0_:c0_ + 2],
                            nb_sb[:, p0_:p0_ + 2], ALU.subtract),
                            rd=[("nb", rk), ("nb", pk)], wr=[("cr", rk)])
                        em.op("s", lambda s2, c0_=c0: s2.activation(
                            cr_sb[:, c0_:c0_ + 2], cr_sb[:, c0_:c0_ + 2], AF.Exp),
                            rd=[("cr", rk)], wr=[("cr", rk)])
                    for h2 in range(2):
                        em.op("s", lambda s2, j_=j, h2_=h2, c0_=c0: s2.activation(
                            p_sb[j_][:, h2_ * 512:(h2_ + 1) * 512],
                            psS[j_][:, h2_ * 512:(h2_ + 1) * 512], AF.Exp,
                            bias=nb_sb[:, c0_ + h2_: c0_ + h2_ + 1],
                            accum_out=rs_sb[:, c0_ + h2_: c0_ + h2_ + 1]),
                            rd=[("psS", (j, h2)), ("nb", rk)],
                            wr=[("p", (j, h2)), ("rs", (rk, h2))])

                def back(i, k, idx):
                    ai = i % 2
                    j = idx % 2
                    rk = idx % 4
                    pk = (idx - 1) % 4
                    c0 = (idx % 4) * 2
                    p0 = ((idx - 1) % 4) * 2
                    k0 = 512 * k
                    for h2 in range(2):
                        if k == 0:
                            em.op("v", lambda v, c0_=c0, h2_=h2: v.tensor_copy(
                                l_sb[:, c0_ + h2_:c0_ + h2_ + 1],
                                rs_sb[:, c0_ + h2_:c0_ + h2_ + 1]),
                                rd=[("rs", (rk, h2))], wr=[("l", (rk, h2))])
                        else:
                            em.op("v", lambda v, c0_=c0, p0_=p0, h2_=h2: v.scalar_tensor_tensor(
                                l_sb[:, c0_ + h2_:c0_ + h2_ + 1],
                                l_sb[:, p0_ + h2_:p0_ + h2_ + 1],
                                cr_sb[:, c0_ + h2_:c0_ + h2_ + 1],
                                rs_sb[:, c0_ + h2_:c0_ + h2_ + 1],
                                ALU.mult, ALU.add),
                                rd=[("l", (pk, h2)), ("cr", rk), ("rs", (rk, h2))],
                                wr=[("l", (rk, h2))])
                    for h2 in range(2):
                        for c in range(4):
                            em.op("t", lambda t, j_=j, h2_=h2, c_=c: t.transpose(
                                psPT[j_][:, h2_ * 512 + c_ * 128: h2_ * 512 + (c_ + 1) * 128],
                                p_sb[j_][:, h2_ * 512 + c_ * 128: h2_ * 512 + (c_ + 1) * 128],
                                ident_bf[:, :]),
                                rd=[("p", (j, h2)), ("identbf", None)],
                                wr=[("psPT", (j, h2, c))])
                    em.op("v", lambda v, j_=j: v.tensor_copy(
                        pt_sb[j_][:, :], psPT[j_][:, :]),
                        rd=[("psPT", (j, None, None))], wr=[("pt", (j, None))])
                    first = True
                    for c in range(4):
                        for h2, (lo, hi) in enumerate(((0, 64), (64, 128))):
                            last = (c == 3) and (h2 == 1)
                            em.op("t", lambda t, j_=j, c_=c, k0_=k0, lo_=lo, hi_=hi, h2_=h2,
                                  st_=first, sp_=last: t.matmul(
                                psAV[j_][:, lo_:hi_],
                                pt_sb[j_][:, h2_ * 512 + c_ * 128: h2_ * 512 + (c_ + 1) * 128],
                                vS[:, (k0_ // 128 + c_) * 128 + lo_:(k0_ // 128 + c_) * 128 + hi_],
                                start=st_, stop=sp_),
                                rd=[("pt", (j, h2)), ("vS", None)], wr=[("psAV", (j,))],
                                wmode=("check" if first else ("record" if last else "skip")))
                            first = False
                    if k == 0:
                        em.op("v", lambda v, j_=j, ai_=ai: v.tensor_copy(
                            acc[ai_][:, :], psAV[j_][:, :]),
                            rd=[("psAV", (j,))], wr=[("acc", ai)])
                    else:
                        for h2, (lo, hi) in enumerate(((0, 64), (64, 128))):
                            em.op("v", lambda v, j_=j, ai_=ai, lo_=lo, hi_=hi, h2_=h2, c0_=c0: v.scalar_tensor_tensor(
                                acc[ai_][:, lo_:hi_], acc[ai_][:, lo_:hi_],
                                cr_sb[:, c0_ + h2_:c0_ + h2_ + 1], psAV[j_][:, lo_:hi_],
                                ALU.mult, ALU.add),
                                rd=[("acc", ai), ("cr", rk), ("psAV", (j,))],
                                wr=[("acc", ai)])

                def finalize(i, lidx):
                    ai = i % 2
                    lc0 = (lidx % 4) * 2
                    lf = lidx % 4
                    em.op("v", lambda v, lc0_=lc0, ai_=ai: v.reciprocal(
                        rl_sb[:, ai_ * 2: ai_ * 2 + 2], l_sb[:, lc0_:lc0_ + 2]),
                        rd=[("l", (lf, 0)), ("l", (lf, 1))], wr=[("rl", ai)])
                    for h2, (lo, hi) in enumerate(((0, 64), (64, 128))):
                        em.op("v", lambda v, i_=i, hp_=hp, lo_=lo, hi_=hi, h2_=h2, ai_=ai: v.scalar_tensor_tensor(
                            x2[:, i_ * D + hp_ * 128 + lo_: i_ * D + hp_ * 128 + hi_],
                            acc[ai_][:, lo_:hi_], rl_sb[:, ai_ * 2 + h2_: ai_ * 2 + h2_ + 1],
                            x2[:, i_ * D + hp_ * 128 + lo_: i_ * D + hp_ * 128 + hi_],
                            ALU.mult, ALU.add),
                            rd=[("acc", ai), ("rl", ai), ("x2", (i, hp))],
                            wr=[("x2", (i, hp))])

                blist = [(i, k) for i in range(NT) for k in range((i + 2) // 2)]
                emit_S(*blist[0], 0)
                emit_S(*blist[1], 1)
                for idx, (i, k) in enumerate(blist):
                    front(i, k, idx)
                    if idx > 0:
                        pi, pk2 = blist[idx - 1]
                        back(pi, pk2, idx - 1)
                        if pk2 == (pi + 2) // 2 - 1:
                            finalize(pi, idx - 1)
                    if idx + 2 < len(blist):
                        emit_S(*blist[idx + 2], idx + 2)
                back(*blist[-1], len(blist) - 1)
                finalize(blist[-1][0], len(blist) - 1)
        em.barrier()
        if _DEBUG:
            em.op("q", lambda q: q.dma_start(
                out=dbg_x2a.rearrange("a p d -> p a d"),
                in_=x2[:, :].rearrange("p (a d) -> p a d", a=NT)),
                rd=[("x2", None)], wr=[("dbgx2a", None)], dma=True)

        with ExitStack() as fctx:
            fsb = lambda n, s_, dt: fctx.enter_context(nc.sbuf_tensor(n, s_, dt))
            xn2T = fsb("xn2T", [128, HC * NT * 128], BF16)
            xn2T_v = xn2T[:, :].rearrange("p (c t) -> p c t", c=HC)

            with ExitStack() as l2ctx:
                l2sb = lambda n, s_, dt: l2ctx.enter_context(nc.sbuf_tensor(n, s_, dt))
                junk2 = l2sb("junk2", [128, D], BF16)
                xnscr2 = l2sb("xnscr2", [128, 2 * D], BF16)
                st2 = l2sb("st2", [128, 16], F32)
                lgt = l2sb("lgt", [128, 2 * 8], F32)
                eqt = l2sb("eqt", [128, 2 * 8], F32)
                mm2 = l2sb("mm2", [128, 2 * 4], F32)
                bst = l2sb("bst", [16, 128], F32)
                psT2 = [l2ctx.enter_context(nc.psum_tensor(f"psT2{j}", [128, 1024], BF16))
                        for j in range(2)]
                psG = [l2ctx.enter_context(nc.psum_tensor(f"psG{j}", [128, 8], F32))
                       for j in range(2)]
                psM = l2ctx.enter_context(nc.psum_tensor("psM", [128, 16], F32))

                for nm, dst in (("b1", b1c), ("b2", b2c), ("bg", bgc)):
                    for e in range(E):
                        em.op("q", lambda q, nm_=nm, e_=e: q.dma_start(
                            out=bst[:, :], in_=din[nm_][e_].rearrange("(c f) -> c f", f=128)),
                            wr=[("bst", None)], dma=True)
                        em.op("t", lambda t: t.transpose(
                            psM[:, :], bst[:, :], ident_f[0:16, 0:16]),
                            rd=[("bst", None), ("identf", None)], wr=[("psM", None)])
                        em.op("v", lambda v, d_=dst, e_=e: v.tensor_copy(
                            d_[:, e_ * 16:(e_ + 1) * 16], psM[:, :]),
                            rd=[("psM", None)], wr=[(nm + "c", e)])

                for i in range(NT):
                    slot = i % 2
                    emit_ln(lambda i_=i: x2[:, i_ * D:(i_ + 1) * D],
                            [("x2", (i, None))], st2, "st2", slot, xnscr2, "xnscr2",
                            psT2, "psT2", xn2T, "xn2T", i, junk2, "junk2")
                    for hc in range(HC):
                        em.op("t", lambda t, hc_=hc, sl_=slot, i_=i: t.matmul(
                            psG[sl_][:, :],
                            xn2T_v[:, hc_, i_ * 128:(i_ + 1) * 128],
                            gw_sb[:, hc_ * 8:(hc_ + 1) * 8],
                            start=(hc_ == 0), stop=False),
                            rd=[("xn2T", i), ("gw", None)], wr=[("psG", (slot,))],
                            wmode=("check" if hc == 0 else "skip"))
                    em.op("t", lambda t, sl_=slot: t.matmul(
                        psG[sl_][:, :], ones_sb[0:1, 0:128], gb_sb[0:1, :],
                        start=False, stop=True),
                        rd=[("ones", None), ("gb", None)], wr=[("psG", (slot,))],
                        wmode="record")
                    g0, m0 = slot * 8, slot * 4
                    em.op("v", lambda v, sl_=slot, g0_=g0: v.tensor_copy(
                        lgt[:, g0_:g0_ + 8], psG[sl_][:, :]),
                        rd=[("psG", (slot,))], wr=[("lgt", (slot,))])
                    em.op("v", lambda v, g0_=g0, m0_=m0: v.reduce_max(
                        mm2[:, m0_:m0_ + 1], lgt[:, g0_:g0_ + 8], axis=AX.X),
                        rd=[("lgt", (slot,))], wr=[("mm2", (slot, 0))])
                    em.op("v", lambda v, g0_=g0, m0_=m0: v.tensor_scalar(
                        eqt[:, g0_:g0_ + 8], lgt[:, g0_:g0_ + 8],
                        mm2[:, m0_:m0_ + 1], None, ALU.is_equal),
                        rd=[("lgt", (slot,)), ("mm2", (slot, 0))], wr=[("eqt", (slot, 0))])
                    em.op("v", lambda v, g0_=g0: v.scalar_tensor_tensor(
                        lgt[:, g0_:g0_ + 8], eqt[:, g0_:g0_ + 8], -1.0e30,
                        lgt[:, g0_:g0_ + 8], ALU.mult, ALU.add),
                        rd=[("eqt", (slot, 0)), ("lgt", (slot,))], wr=[("lgt", (slot,))])
                    em.op("v", lambda v, g0_=g0, m0_=m0: v.reduce_max(
                        mm2[:, m0_ + 1:m0_ + 2], lgt[:, g0_:g0_ + 8], axis=AX.X),
                        rd=[("lgt", (slot,))], wr=[("mm2", (slot, 1))])
                    em.op("v", lambda v, m0_=m0: v.tensor_tensor(
                        mm2[:, m0_ + 2:m0_ + 3], mm2[:, m0_:m0_ + 1],
                        mm2[:, m0_ + 1:m0_ + 2], ALU.subtract),
                        rd=[("mm2", (slot, 0)), ("mm2", (slot, 1))], wr=[("mm2", (slot, 2))])
                    em.op("v", lambda v, m0_=m0: v.tensor_scalar(
                        mm2[:, m0_ + 3:m0_ + 4], mm2[:, m0_ + 2:m0_ + 3], -1.0, None, ALU.mult),
                        rd=[("mm2", (slot, 2))], wr=[("mm2", (slot, 3))])
                    em.op("s", lambda s2, m0_=m0: s2.activation(
                        mm2[:, m0_ + 2:m0_ + 4], mm2[:, m0_ + 2:m0_ + 4], AF.Sigmoid),
                        rd=[("mm2", (slot, 2)), ("mm2", (slot, 3))],
                        wr=[("mm2", (slot, 2)), ("mm2", (slot, 3))])
                    em.op("v", lambda v, g0_=g0, m0_=m0: v.tensor_scalar(
                        eqt[:, g0_:g0_ + 8], eqt[:, g0_:g0_ + 8],
                        mm2[:, m0_ + 2:m0_ + 3], None, ALU.mult),
                        rd=[("eqt", (slot, 0)), ("mm2", (slot, 2))], wr=[("eqt", (slot, 0))])
                    em.op("v", lambda v, g0_=g0, m0_=m0: v.tensor_scalar(
                        lgt[:, g0_:g0_ + 8], lgt[:, g0_:g0_ + 8],
                        mm2[:, m0_ + 1:m0_ + 2], None, ALU.is_equal),
                        rd=[("lgt", (slot,)), ("mm2", (slot, 1))], wr=[("lgt", (slot,))])
                    em.op("v", lambda v, g0_=g0, m0_=m0, i_=i: v.scalar_tensor_tensor(
                        we_sb[:, i_ * 8:(i_ + 1) * 8], lgt[:, g0_:g0_ + 8],
                        mm2[:, m0_ + 3:m0_ + 4], eqt[:, g0_:g0_ + 8],
                        ALU.mult, ALU.add),
                        rd=[("lgt", (slot,)), ("mm2", (slot, 3)), ("eqt", (slot, 0))],
                        wr=[("we", i)])
            em.barrier()
            if _DEBUG:
                em.op("q", lambda q: q.dma_start(out=dbg_we, in_=we_sb[:, :]),
                      rd=[("we", None)], wr=[("dbgwe", None)], dma=True)

            if _DEBUG:
                em.op("q", lambda q: q.dma_start(out=dbg_xn2, in_=xn2T[:, :]),
                      rd=[("xn2T", None)], wr=[("dbgxn2", None)], dma=True)
            z_sb = fsb("z_sb", [128, FC * 1024], BF16)
            s_sb = fsb("s_sb", [128, GC * 1024], BF16)
            w1_sl = [fsb(f"w1sl{j}", [128, HC * 128], BF16) for j in range(2)]
            w2_sl = [fsb(f"w2sl{j}", [128, HC * 128], BF16) for j in range(2)]
            wg_sl = [fsb(f"wgsl{j}", [128, FC * 128], BF16) for j in range(2)]
            w3_sl = [fsb(f"w3sl{j}", [128, GC * 256], BF16) for j in range(2)]
            x2t = fsb("x2t", [128, 2 * 512], BF16)

            psA1 = [fctx.enter_context(nc.psum_tensor(f"psA1{j}", [128, 512], F32))
                    for j in range(2)]
            psA2 = [fctx.enter_context(nc.psum_tensor(f"psA2{j}", [128, 512], F32))
                    for j in range(2)]
            psB = [fctx.enter_context(nc.psum_tensor(f"psB{j}", [128, 512], F32))
                   for j in range(2)]
            psC = [fctx.enter_context(nc.psum_tensor(f"psC{j}", [128, 256], F32))
                   for j in range(2)]

            def load_w1w2(e, fc):
                j = fc % 2
                for nm, buf, sname in (("w1s", w1_sl[j], "w1"), ("w2s", w2_sl[j], "w2")):
                    em.op("q", lambda q, b_=buf, sn_=sname, e_=e, fc_=fc: q.dma_start(
                        out=b_[:, :].rearrange("p (c f) -> p c f", c=HC),
                        in_=scr[sn_][e_].rearrange("(c p) f -> p c f", p=128)[
                            :, :, fc_ * 128:(fc_ + 1) * 128]),
                        rd=[("scr_" + sname, e)], wr=[(nm, j)], dma=True)

            def load_wg(e, gc):
                j = gc % 2
                for hh in range(2):
                    em.op("q", lambda q, e_=e, gc_=gc, j_=j, h_=hh: q.dma_start(
                        out=wg_sl[j_][:, :].rearrange("p (c f) -> p c f", c=FC)[
                            :, h_ * 8:(h_ + 1) * 8, :],
                        in_=scr["wg"][e_].rearrange("(c p) f -> p c f", p=128)[
                            :, h_ * 8:(h_ + 1) * 8, gc_ * 128:(gc_ + 1) * 128]),
                        rd=[("scr_wg", e)], wr=[("wgs", (j, hh))], dma=True)

            def load_w3(e, qh):
                j = qh % 2
                for hh in range(2):
                    em.op("q", lambda q, e_=e, qh_=qh, j_=j, h_=hh: q.dma_start(
                        out=w3_sl[j_][:, :].rearrange("p (c f) -> p c f", c=GC)[
                            :, h_ * 8:(h_ + 1) * 8, :],
                        in_=scr["w3"][e_].rearrange("(c p) f -> p c f", p=128)[
                            :, h_ * 8:(h_ + 1) * 8, qh_ * 256:(qh_ + 1) * 256]),
                        rd=[("scr_w3", e)], wr=[("w3s", (j, hh))], dma=True)

            def load_b3(e, row):
                em.op("g", lambda g, e_=e, r_=row: g.dma_start(
                    out=b3rows[r_:r_ + 1, :], in_=din["b3"][e_:e_ + 1, :]),
                    wr=[("b3row", row)], dma=True)

            ROWS = (0, 32, 64)
            load_b3(0, ROWS[0])
            for half in range(2):
                for e in range(E):
                    it = half * E + e
                    row = ROWS[it % 3]
                    if it + 1 < 2 * E:
                        load_b3((it + 1) % E, ROWS[(it + 1) % 3])
                    load_w1w2(e, 0)
                    load_w1w2(e, 1)
                    for fc in range(FC):
                        j = fc % 2
                        for tb in range(2):
                            jj = (fc * 2 + tb) % 2
                            for psA, wsl, wkey in ((psA1, w1_sl, "w1s"), (psA2, w2_sl, "w2s")):
                                for hc in range(HC):
                                    em.op("t", lambda t, hc_=hc, j_=j, jj_=jj, tb_=tb, h_=half,
                                          ps_=psA, w_=wsl: t.matmul(
                                        ps_[jj_][:, :],
                                        w_[j_][:, hc_ * 128:(hc_ + 1) * 128],
                                        xn2T_v[:, hc_, h_ * 1024 + tb_ * 512: h_ * 1024 + (tb_ + 1) * 512],
                                        start=(hc_ == 0), stop=(hc_ == HC - 1)),
                                        rd=[(wkey, j), ("xn2T", None)],
                                        wr=[("psA1" if psA is psA1 else "psA2", (jj,))],
                                        wmode=("check" if hc == 0 else ("record" if hc == HC - 1 else "skip")))
                            em.op("s", lambda s2, jj_=jj, e_=e, fc_=fc: s2.activation(
                                x2t[:, jj_ * 512:(jj_ + 1) * 512], psA2[jj_][:, :], AF.Identity,
                                bias=b2c[:, e_ * FC + fc_: e_ * FC + fc_ + 1]),
                                rd=[("psA2", (jj,)), ("b2c", e)], wr=[("x2t", jj)])
                            em.op("v", lambda v, jj_=jj, e_=e, fc_=fc, tb_=tb: v.scalar_tensor_tensor(
                                z_sb[:, fc_ * 1024 + tb_ * 512: fc_ * 1024 + (tb_ + 1) * 512],
                                psA1[jj_][:, :], b1c[:, e_ * FC + fc_: e_ * FC + fc_ + 1],
                                x2t[:, jj_ * 512:(jj_ + 1) * 512], ALU.add, ALU.mult),
                                rd=[("psA1", (jj,)), ("b1c", e), ("x2t", jj)],
                                wr=[("z", (fc, tb))])
                        if fc + 2 < FC:
                            load_w1w2(e, fc + 2)
                        elif fc == FC - 2:
                            load_wg(e, 0)
                            load_wg(e, 1)
                    if _DEBUG and half == 0 and e == 0:
                        em.op("q", lambda q: q.dma_start(out=dbg_z, in_=z_sb[:, :]),
                              rd=[("z", None)], wr=[("dbgz", None)], dma=True)
                    for gc in range(GC):
                        j = gc % 2
                        for tb in range(2):
                            jj = (gc * 2 + tb) % 2
                            for fc in range(FC):
                                em.op("t", lambda t, fc_=fc, j_=j, jj_=jj, tb_=tb: t.matmul(
                                    psB[jj_][:, :],
                                    wg_sl[j_][:, fc_ * 128:(fc_ + 1) * 128],
                                    z_sb[:, fc_ * 1024 + tb_ * 512: fc_ * 1024 + (tb_ + 1) * 512],
                                    start=(fc_ == 0), stop=(fc_ == FC - 1)),
                                    rd=[("wgs", (j, fc // 8)), ("z", (fc, tb))],
                                    wr=[("psB", (jj,))],
                                    wmode=("check" if fc == 0 else ("record" if fc == FC - 1 else "skip")))
                            em.op("s", lambda s2, jj_=jj, e_=e, gc_=gc, tb_=tb: s2.activation(
                                s_sb[:, gc_ * 1024 + tb_ * 512: gc_ * 1024 + (tb_ + 1) * 512],
                                psB[jj_][:, :], AF.Silu,
                                bias=bgc[:, e_ * GC + gc_: e_ * GC + gc_ + 1]),
                                rd=[("psB", (jj,)), ("bgc", e)], wr=[("s", (gc, tb))])
                        if gc + 2 < GC:
                            load_wg(e, gc + 2)
                        elif gc == GC - 2:
                            load_w3(e, 0)
                            load_w3(e, 1)
                    if _DEBUG and half == 0 and e == 0:
                        em.op("q", lambda q: q.dma_start(out=dbg_s, in_=s_sb[:, :]),
                              rd=[("s", None)], wr=[("dbgs", None)], dma=True)
                    for qh in range(4):
                        j = qh % 2
                        for tt in range(8):
                            jj = (qh * 8 + tt) % 2
                            em.op("t", lambda t, jj_=jj, r_=row, qh_=qh: t.matmul(
                                psC[jj_][:, :], ones_sb[r_:r_ + 1, 0:128],
                                b3rows[r_:r_ + 1, qh_ * 256:(qh_ + 1) * 256],
                                start=True, stop=False),
                                rd=[("ones", None), ("b3row", row)], wr=[("psC", (jj,))],
                                wmode="check")
                            for gc in range(GC):
                                em.op("t", lambda t, gc_=gc, j_=j, jj_=jj, tt_=tt: t.matmul(
                                    psC[jj_][:, :],
                                    s_sb[:, gc_ * 1024 + tt_ * 128: gc_ * 1024 + (tt_ + 1) * 128],
                                    w3_sl[j_][:, gc_ * 256:(gc_ + 1) * 256],
                                    start=False, stop=(gc_ == GC - 1)),
                                    rd=[("s", (gc, tt // 4)), ("w3s", (j, gc // 8))],
                                    wr=[("psC", (jj,))],
                                    wmode=("record" if gc == GC - 1 else "skip"))
                            i_glob = half * 8 + tt
                            em.op("v", lambda v, jj_=jj, i_=i_glob, qh_=qh, e_=e: v.scalar_tensor_tensor(
                                x2[:, i_ * D + qh_ * 256: i_ * D + (qh_ + 1) * 256],
                                psC[jj_][:, :], we_sb[:, i_ * 8 + e_: i_ * 8 + e_ + 1],
                                x2[:, i_ * D + qh_ * 256: i_ * D + (qh_ + 1) * 256],
                                ALU.mult, ALU.add),
                                rd=[("psC", (jj,)), ("we", i_glob), ("x2", (i_glob, ("f", qh)))],
                                wr=[("x2", (i_glob, ("f", qh)))])
                        if qh + 2 < 4:
                            load_w3(e, qh + 2)
        em.barrier()

        if _DEBUG:
            em.op("q", lambda q: q.dma_start(
                out=dbg_x2e.rearrange("a p d -> p a d"),
                in_=x2[:, :].rearrange("p (a d) -> p a d", a=NT)),
                rd=[("x2", None)], wr=[("dbgx2e", None)], dma=True)
        em.op("q", lambda q: q.dma_start(
            out=out_hbm.rearrange("a p d -> p a d"),
            in_=x2[:, :].rearrange("p (a d) -> p a d", a=NT)),
            rd=[("x2", None)], wr=[("outhbm", None)], dma=True)
        em.wait_all("q")

        with nc.Block() as block:
            em.build(block)
    return nc


_CACHE = {}


def _get_program():
    if "nc" not in _CACHE:
        _CACHE["nc"] = build_program()
    return _CACHE["nc"]


def kernel(**inputs):
    x = np.asarray(inputs["L0_attn_x"], dtype=np.float32)
    wqkv = np.asarray(inputs["L0_attn_qkv_weights"], dtype=np.float32)
    gate_w = np.asarray(inputs["L0_ffn_gate_w"], dtype=np.float32)
    gate_b = np.asarray(inputs["L0_ffn_gate_b"], dtype=np.float32).reshape(1, E)
    feed = {
        "wqkv": wqkv, "gate_w": gate_w, "gate_b": gate_b,
        "w1": np.asarray(inputs["L0_ffn_w1"], dtype=np.float32),
        "b1": np.asarray(inputs["L0_ffn_b1"], dtype=np.float32),
        "w2": np.asarray(inputs["L0_ffn_w2"], dtype=np.float32),
        "b2": np.asarray(inputs["L0_ffn_b2"], dtype=np.float32),
        "wg": np.asarray(inputs["L0_ffn_wg"], dtype=np.float32),
        "bg": np.asarray(inputs["L0_ffn_bg"], dtype=np.float32),
        "w3": np.asarray(inputs["L0_ffn_w3"], dtype=np.float32),
        "b3": np.asarray(inputs["L0_ffn_b3"], dtype=np.float32),
        "ident_bf": np.eye(128, dtype=ml_dtypes.bfloat16),
        "ident_f": np.eye(128, dtype=np.float32),
    }
    masks = []
    for p in range(2):
        cols = np.arange(256)[None, :]
        rows = np.arange(128)[:, None]
        tri = np.where(cols <= 128 * p + rows, 0.0, NEG).astype(np.float32)
        neg = np.full((128, 256), NEG, np.float32)
        zer = np.zeros((128, 256), np.float32)
        m0 = np.concatenate([tri, neg], axis=1)
        m1 = np.concatenate([zer, tri], axis=1)
        masks.append(np.concatenate([m0, m1], axis=1))

    nc = _get_program()
    in_maps = []
    for c in range(8):
        b, p = c >> 1, c & 1
        m = dict(feed)
        m["x_seq"] = np.ascontiguousarray(x[b])
        m["x_own"] = np.ascontiguousarray(x[b].reshape(NT, 2, 128, D)[:, p])
        m["mask"] = masks[p]
        in_maps.append(m)
    res = run_bass_kernel_spmd(nc, in_maps, core_ids=list(range(8)))
    out = np.empty((B, S, D), dtype=np.float32)
    ov = out.reshape(B, NT, 2, 128, D)
    for c in range(8):
        ov[c >> 1, :, c & 1] = res.results[c]["out"]
    return out
